# version 06
# baseline (speedup 1.0000x reference)
import contextlib
import numpy as np
import ml_dtypes

import concourse.bass as bass
import concourse.mybir as mybir
from concourse.bass_utils import run_bass_kernel_spmd

F32 = mybir.dt.float32
BF16 = mybir.dt.bfloat16
I32 = mybir.dt.int32
AF = mybir.ActivationFunctionType
ALU = mybir.AluOpType
AX = mybir.AxisListType.X

E = 1024
H = 16
NE = E // 128

DEBUG_OUTPUTS = False
STOP_AFTER = None
A3_SB = None
A3_PHASE = 9


class Seq:

    def __init__(self, nc, name):
        self.nc = nc
        self.name = name
        self.steps = []
        self.extra = {}

    def step(self, eng, fn, dma=False):
        self.steps.append((eng, fn, 16 if dma else 1))
        return len(self.steps) - 1

    def after_counts(self, sid):
        cc = cd = 0
        for i in range(sid + 1):
            if self.steps[i][2] == 16:
                cd += 16
            else:
                cc += 1
        return cc, cd

    def emit(self, es):
        nc = self.nc
        semc = es.enter_context(nc.semaphore(f"{self.name}_sc"))
        semd = es.enter_context(nc.semaphore(f"{self.name}_sd"))
        prec, pred = [], []
        cc = cd = 0
        for (_, _, inc) in self.steps:
            prec.append(cc)
            pred.append(cd)
            if inc == 16:
                cd += 16
            else:
                cc += 1
        blk = es.enter_context(nc.Block(self.name))
        resolver = (semc, semd, prec, pred)
        for engname in ("sync", "gpsimd", "vector", "scalar", "tensor"):
            mysteps = [(i, f, inc) for i, (en, f, inc) in enumerate(self.steps)
                       if en == engname]
            if not mysteps and engname not in self.extra:
                continue

            def body(eng, mysteps=mysteps, engname=engname):
                if engname in self.extra:
                    self.extra[engname](eng, resolver)
                for i, f, inc in mysteps:
                    if prec[i] > 0:
                        eng.wait_ge(semc, prec[i])
                    if pred[i] > 0:
                        eng.wait_ge(semd, pred[i])
                    instr = f(eng)
                    instr.then_inc(semd if inc == 16 else semc, inc)
                if cc > 0:
                    eng.wait_ge(semc, cc)
                if cd > 0:
                    eng.wait_ge(semd, cd)
            getattr(blk, engname)(body)
        return semc, semd


class PB:

    def __init__(self, nc, es, name, engines):
        self.nc = nc
        self.name = name
        self.engines = list(engines)
        self.sem = {e: es.enter_context(nc.semaphore(f"{name}_{e}"))
                    for e in self.engines}
        self.bodies = {}
        self.marks = {}
        self._es = es

    def body(self, engname):
        def deco(fn):
            self.bodies[engname] = fn
            return fn
        return deco

    def dma_sem(self, nm, n=1):
        if n == 1:
            return self._es.enter_context(
                self.nc.semaphore(f"{self.name}_{nm}"))
        return [self._es.enter_context(
            self.nc.semaphore(f"{self.name}_{nm}{i}")) for i in range(n)]

    def finish(self):
        class Ctx:
            def __init__(c, pb, eng_name, dry, eng):
                c.pb, c.en, c.dry, c.eng = pb, eng_name, dry, eng
                c.n = 0

            def op(c, fn):
                if not c.dry:
                    fn(c.eng).then_inc(c.pb.sem[c.en], 1)
                c.n += 1

            def opm(c, fn):
                c.op(fn)
                return c.n

            def mark(c, *key):
                if c.dry:
                    c.pb.marks[(c.en,) + key] = c.n

            def at(c, en2, *key):
                return c.pb.marks[(en2,) + key]

            def wait(c, en2, *key):
                if not c.dry:
                    v = c.pb.marks[(en2,) + key]
                    if v > 0:
                        c.eng.wait_ge(c.pb.sem[en2], v)

            def wait_n(c, en2, v):
                if not c.dry and v > 0:
                    c.eng.wait_ge(c.pb.sem[en2], v)

            def raw(c, fn):
                if not c.dry:
                    fn(c.eng)

            def dma(c, sem, fn):
                if not c.dry:
                    fn(c.eng).then_inc(sem, 16)

        for en, fn in self.bodies.items():
            fn(Ctx(self, en, True, None))
        blk = self._es.enter_context(self.nc.Block(self.name))
        for en, fn in self.bodies.items():
            def run(eng, fn=fn, en=en):
                fn(Ctx(self, en, False, eng))
            getattr(blk, en)(run)


def build_program(S):
    KC = S // 128
    NSB = S // 512

    nc = bass.Bass("TRN2", target_bir_lowering=False, debug=False)

    x_in = nc.dram_tensor("x", [S, E], F32, kind="ExternalInput").ap()
    w_in = {nm: nc.dram_tensor(nm, [E, E], F32, kind="ExternalInput").ap()
            for nm in ("wk", "wv", "wq", "wo")}
    bqd_in = nc.dram_tensor("bqd", [128, H], F32, kind="ExternalInput").ap()
    bkd_in = nc.dram_tensor("bkd", [128, H], F32, kind="ExternalInput").ap()
    bvc_in = nc.dram_tensor("bvc", [128, NE], F32, kind="ExternalInput").ap()
    bdm_in = nc.dram_tensor("bdm", [128, 512], BF16, kind="ExternalInput").ap()
    ii_in = nc.dram_tensor("iic", [128, 64], BF16, kind="ExternalInput").ap()
    bo_in = nc.dram_tensor("bo", [1, E], F32, kind="ExternalInput").ap()
    packed_in = nc.dram_tensor("packed", [256, E], I32, kind="ExternalInput").ap()
    ws_in = nc.dram_tensor("ws", [1, 1], F32, kind="ExternalInput").ap()
    fbias_in = nc.dram_tensor("fbias", [1, E], F32, kind="ExternalInput").ap()
    ccol_in = nc.dram_tensor("ccol", [128, NE], F32, kind="ExternalInput").ap()
    wmod_in = nc.dram_tensor("wmod", [E, 6 * E], F32, kind="ExternalInput").ap()
    ident_in = nc.dram_tensor("identc", [128, 128], BF16, kind="ExternalInput").ap()
    onesb_in = nc.dram_tensor("onesbc", [128, 128], BF16, kind="ExternalInput").ap()
    onesf_in = nc.dram_tensor("onesfc", [128, 128], F32, kind="ExternalInput").ap()
    out_hbm = nc.dram_tensor("out", [S, E], F32, kind="ExternalOutput").ap()

    _scr_kind = "ExternalOutput" if DEBUG_OUTPUTS else "Internal"
    wT_scr = {nm: nc.dram_tensor(f"{nm}T_scr", [E, E], BF16, kind="Internal").ap()
              for nm in ("wk", "wv", "wq", "wo")}
    xnew_scr = nc.dram_tensor("xnew_scr", [S, E], F32, kind=_scr_kind).ap()
    o5_scr = nc.dram_tensor("o5_scr", [S * 16, 64], BF16, kind=_scr_kind)
    modrow_scr = nc.dram_tensor("modrow_scr", [1, 6 * E], F32,
                                kind=_scr_kind).ap()

    with contextlib.ExitStack() as es0:
        t = {}
        for nm, shape, dt in (
                ("ident", [128, 128], BF16), ("onesb", [128, 128], BF16),
                ("onesf", [128, 128], F32),
                ("bqd", [128, H], F32), ("bkd", [128, H], F32),
                ("bvc", [128, NE], F32), ("bdm", [128, 512], BF16),
                ("iic", [128, 64], BF16),
                ("wssb", [1, 1], F32), ("ws_rep", [128, 1], F32),
                ("ca1_rep", [128, E], F32), ("sa_rep", [128, E], F32)):
            t[nm] = es0.enter_context(nc.sbuf_tensor("t_" + nm, shape, dt))

        emit_b0(nc, t, ident_in, onesb_in, onesf_in, bqd_in,
                bkd_in, bvc_in, ws_in, bdm_in, ii_in)
        emit_b1(nc, t, w_in, wT_scr)
        emit_b2(nc, t, ccol_in, wmod_in, modrow_scr)
        if STOP_AFTER == "b2":
            return nc
        emit_b3atn(nc, t, S, x_in, wT_scr, o5_scr)
        if STOP_AFTER == "a3":
            return nc
        with contextlib.ExitStack() as es3:
            WoTga = [es3.enter_context(
                nc.sbuf_tensor(f"WoTga{j}", [128, E], BF16))
                for j in range(NE)]
            gabo_bf = es3.enter_context(
                nc.sbuf_tensor("gabo_bf", [1, E], BF16))
            emit_b5(nc, t, wT_scr, modrow_scr, bo_in, WoTga, gabo_bf)
            emit_b6op(nc, t, S, x_in, o5_scr, WoTga, gabo_bf, xnew_scr)
        if STOP_AFTER == "b6":
            return nc

        emit_b7b8(nc, t, S, packed_in, fbias_in, modrow_scr, xnew_scr,
                  out_hbm)

    return nc


def emit_b0(nc, t, ident_in, onesb_in, onesf_in, bqd_in, bkd_in,
            bvc_in, ws_in, bdm_in, ii_in):
    q = Seq(nc, "B0")
    q.step("sync", lambda e: e.dma_start(out=t["ident"][:], in_=ident_in), dma=True)
    q.step("sync", lambda e: e.dma_start(out=t["onesb"][:], in_=onesb_in), dma=True)
    q.step("sync", lambda e: e.dma_start(out=t["onesf"][:], in_=onesf_in), dma=True)
    q.step("sync", lambda e: e.dma_start(out=t["bqd"][:], in_=bqd_in), dma=True)
    q.step("sync", lambda e: e.dma_start(out=t["bkd"][:], in_=bkd_in), dma=True)
    q.step("sync", lambda e: e.dma_start(out=t["bvc"][:], in_=bvc_in), dma=True)
    q.step("sync", lambda e: e.dma_start(out=t["wssb"][:], in_=ws_in), dma=True)
    q.step("sync", lambda e: e.dma_start(out=t["bdm"][:], in_=bdm_in), dma=True)
    q.step("sync", lambda e: e.dma_start(out=t["iic"][:], in_=ii_in), dma=True)
    with contextlib.ExitStack() as es:
        q.emit(es)


def emit_b1(nc, t, w_in, wT_scr):
    with contextlib.ExitStack() as es:
        wstg = [es.enter_context(nc.sbuf_tensor(f"b1s{i}", [128, E], F32))
                for i in range(2)]
        wbf = [es.enter_context(nc.sbuf_tensor(f"b1b{i}", [128, E], BF16))
               for i in range(8)]
        wout = [es.enter_context(nc.sbuf_tensor(f"b1o{i}", [128, E], BF16))
                for i in range(2)]
        ps_tr = [es.enter_context(nc.psum_tensor(f"b1p{i}", [128, E], F32))
                 for i in range(2)]
        s_ld = [es.enter_context(nc.semaphore(f"b1_ld{i}")) for i in range(2)]

        q = Seq(nc, "B1")
        worder = ("wk", "wv", "wq", "wo")
        cvt_ids = {}
        for wi, nm in enumerate(worder):
            for tt in range(8):
                gi = wi * 8 + tt
                def cvt(e, gi=gi, tt=tt):
                    e.wait_ge(s_ld[gi % 2], 16 * (gi // 2 + 1))
                    return e.tensor_copy(wbf[tt][:], wstg[gi % 2][:])
                cvt_ids[gi] = q.step("vector", cvt)
            for j in range(NE):
                def trs(e, j=j):
                    for tt in range(8):
                        mm = e.matmul(
                            ps_tr[j % 2][:, 128 * tt:128 * (tt + 1)],
                            wbf[tt][:, 128 * j:128 * (j + 1)], t["ident"][:],
                            start=True, stop=True)
                    return mm
                q.step("tensor", trs)
                q.step("scalar", lambda e, j=j: e.copy(wout[j % 2][:],
                                                       ps_tr[j % 2][:]))
                q.step("gpsimd", lambda e, nm=nm, j=j: e.dma_start(
                    out=wT_scr[nm][128 * j:128 * (j + 1), :],
                    in_=wout[j % 2][:]), dma=True)

        def loads(eng, resolver):
            semc = resolver[0]
            for wi, nm in enumerate(worder):
                for tt in range(8):
                    gi = wi * 8 + tt
                    if gi >= 2:
                        eng.wait_ge(semc, q.after_counts(cvt_ids[gi - 2])[0])
                    eng.dma_start(out=wstg[gi % 2][:],
                                  in_=w_in[nm][128 * tt:128 * (tt + 1), :]
                                  ).then_inc(s_ld[gi % 2], 16)
        q.extra["sync"] = loads
        q.emit(es)


def emit_b2(nc, t, ccol_in, wmod_in, modrow_scr):
    NCH = 6 * E // 512
    with contextlib.ExitStack() as es:
        wmb = [es.enter_context(nc.sbuf_tensor(f"b2w{i}", [128, 512], F32))
               for i in range(2)]
        csil = es.enter_context(nc.sbuf_tensor("b2c", [128, NE], F32))
        ctmp = es.enter_context(nc.sbuf_tensor("b2ct", [128, NE], F32))
        modrow = es.enter_context(nc.sbuf_tensor("b2m", [1, 6 * E], F32))
        vrow = es.enter_context(nc.sbuf_tensor("b2v", [1, E], F32))
        ps_mod = [es.enter_context(nc.psum_tensor(f"b2pm{i}", [1, 512], F32))
                  for i in range(2)]
        ps_bc = es.enter_context(nc.psum_tensor("b2pb", [128, 512], F32))
        s_ld = [es.enter_context(nc.semaphore(f"b2_ld{i}")) for i in range(2)]

        q = Seq(nc, "B2")
        q.step("sync", lambda e: e.dma_start(out=csil[:], in_=ccol_in), dma=True)
        q.step("scalar", lambda e: e.activation(ctmp[:], csil[:], AF.Exp,
                                                bias=0.0, scale=-1.0))
        q.step("vector", lambda e: e.tensor_scalar(ctmp[:], ctmp[:], 1.0,
                                                   None, ALU.add))
        q.step("vector", lambda e: e.reciprocal(ctmp[:], ctmp[:]))
        q.step("vector", lambda e: e.tensor_mul(csil[:], csil[:], ctmp[:]))
        mm_ids = {}
        for ch in range(NCH):
            for e_ in range(NE):
                idx = ch * NE + e_

                def mm(eng, ch=ch, e_=e_, idx=idx):
                    eng.wait_ge(s_ld[idx % 2], 16 * (idx // 2 + 1))
                    return eng.matmul(ps_mod[ch % 2][0:1, :],
                                      csil[:, e_:e_ + 1], wmb[idx % 2][:],
                                      start=(e_ == 0), stop=(e_ == NE - 1))
                mm_ids[idx] = q.step("tensor", mm)
            q.step("vector", lambda e, ch=ch: e.tensor_copy(
                modrow[0:1, 512 * ch:512 * (ch + 1)], ps_mod[ch % 2][0:1, :]))
        q.step("vector", lambda e: e.tensor_scalar(
            vrow[0:1, :], modrow[0:1, E:2 * E], 1.0, None, ALU.add))
        for half in range(2):
            q.step("tensor", lambda e, h=half: e.matmul(
                ps_bc[:, 0:512], t["onesf"][0:1, :],
                vrow[0:1, 512 * h:512 * (h + 1)], start=True, stop=True))
            q.step("vector", lambda e, h=half: e.tensor_copy(
                t["ca1_rep"][:, 512 * h:512 * (h + 1)], ps_bc[:, 0:512]))
        q.step("vector", lambda e: e.tensor_copy(vrow[0:1, :],
                                                 modrow[0:1, 0:E]))
        for half in range(2):
            q.step("tensor", lambda e, h=half: e.matmul(
                ps_bc[:, 0:512], t["onesf"][0:1, :],
                vrow[0:1, 512 * h:512 * (h + 1)], start=True, stop=True))
            q.step("vector", lambda e, h=half: e.tensor_copy(
                t["sa_rep"][:, 512 * h:512 * (h + 1)], ps_bc[:, 0:512]))
        q.step("tensor", lambda e: e.matmul(ps_bc[:, 0:1], t["onesf"][0:1, :],
                                            t["wssb"][0:1, 0:1],
                                            start=True, stop=True))
        q.step("vector", lambda e: e.tensor_copy(t["ws_rep"][:],
                                                 ps_bc[:, 0:1]))
        q.step("sync", lambda e: e.dma_start(out=modrow_scr, in_=modrow[:]),
               dma=True)

        def loads(eng, resolver):
            semc = resolver[0]
            for idx in range(NCH * NE):
                if idx >= 2:
                    eng.wait_ge(semc, q.after_counts(mm_ids[idx - 2])[0])
                ch, e_ = idx // NE, idx % NE
                eng.dma_start(out=wmb[idx % 2][:],
                              in_=wmod_in[128 * e_:128 * (e_ + 1),
                                          512 * ch:512 * (ch + 1)]
                              ).then_inc(s_ld[idx % 2], 16)
        q.extra["gpsimd"] = loads
        q.emit(es)


def emit_b3atn(nc, t, S, x_in, wT_scr, o5_scr):
    NSB = S // 512
    if A3_SB is not None:
        NSB = min(A3_SB, NSB)
    NU = 56 if A3_PHASE >= 5 else 40
    with contextlib.ExitStack() as es:
        WqD = [es.enter_context(nc.sbuf_tensor(f"a3wq{j}", [128, 2 * E], BF16))
               for j in range(NE)]
        WkD = [es.enter_context(nc.sbuf_tensor(f"a3wk{j}", [128, 2 * E], BF16))
               for j in range(NE)]
        WvT = [es.enter_context(nc.sbuf_tensor(f"a3wv{j}", [128, E], BF16))
               for j in range(NE)]
        xb = [es.enter_context(nc.sbuf_tensor(f"a3x{i}", [128, E], F32))
              for i in range(2)]
        hb = [es.enter_context(nc.sbuf_tensor(f"a3h{i}", [128, E], BF16))
              for i in range(4)]
        hT = [es.enter_context(nc.sbuf_tensor(f"a3ht{j}", [128, 512], BF16))
              for j in range(NE)]
        lnx = [es.enter_context(nc.sbuf_tensor(f"a3ln{i}", [128, E], F32))
               for i in range(2)]
        sqd = [es.enter_context(nc.sbuf_tensor(f"a3sqd{i}", [128, E], BF16))
               for i in range(2)]
        st_ = [es.enter_context(nc.sbuf_tensor(f"a3st{i}", [128, 12], F32))
               for i in range(2)]
        QD = es.enter_context(nc.sbuf_tensor("a3qd", [128, 16 * 512], BF16))
        KD = es.enter_context(nc.sbuf_tensor("a3kd", [128, 16 * 512], BF16))
        VP = es.enter_context(nc.sbuf_tensor("a3vp", [128, 8 * 512], BF16))
        eS = [es.enter_context(nc.sbuf_tensor(f"a3es{i}", [128, 512], BF16))
              for i in range(16)]
        V4 = [es.enter_context(nc.sbuf_tensor(f"a3v4_{i}", [128, 260], BF16))
              for i in range(16)]
        o5s = [es.enter_context(nc.sbuf_tensor(f"a3o5_{i}", [128, 512], BF16))
               for i in range(2)]
        rec = es.enter_context(nc.sbuf_tensor("a3rec", [128, 16], F32))
        ps_tr = [es.enter_context(nc.psum_tensor(f"a3pt{i}", [128, 512], F32))
                 for i in range(2)]
        ps_a = [es.enter_context(nc.psum_tensor(f"a3pa{i}", [128, 512], F32))
                for i in range(2)]
        ps_g = [es.enter_context(nc.psum_tensor(f"a3pg{i}", [128, 512], F32))
                for i in range(2)]
        ps_av = [es.enter_context(nc.psum_tensor(f"a3pv{i}", [128, 260], F32))
                 for i in range(2)]

        pb = PB(nc, es, "A3", ["sync", "gpsimd", "vector", "scalar", "tensor"])
        sx = pb.dma_sem("sx", 2)
        swt = pb.dma_sem("swt")
        so5 = pb.dma_sem("so5", 2)

        def pa_user(sb, u):
            if u < 16:
                return ("scalar", ("pe", 56 * sb + u))
            if u < 32:
                return ("scalar", ("pe", 56 * sb + u))
            if u < 40:
                return ("vector", ("ve", 56 * sb + u))
            return ("vector", ("v4e", 56 * sb + u))

        @pb.body("sync")
        def _(c):
            for nm, W in (("wq", WqD), ("wk", WkD)):
                for j in range(NE):
                    for rep in range(2):
                        def wdup(e, nm=nm, W=W, j=j, rep=rep):
                            dst = bass.AP(W[j], 64 * rep,
                                          [[2 * E, 128], [128, H], [1, 64]])
                            src = bass.AP(wT_scr[nm].tensor, 128 * j * E,
                                          [[E, 128], [64, H], [1, 64]])
                            return e.dma_start(out=dst, in_=src)
                        c.dma(swt, wdup)
            for j in range(NE):
                c.dma(swt, lambda e, j=j: e.dma_start(
                    out=WvT[j][:],
                    in_=wT_scr["wv"][128 * j:128 * (j + 1), :]))
            for sb in range(NSB):
                for tt in range(4):
                    xi = 4 * sb + tt
                    if xi >= 2:
                        c.wait("scalar", "lnx", xi - 2)
                        c.wait("vector", "v1", xi - 2)
                    c.dma(sx[xi % 2], lambda e, sb=sb, tt=tt, xi=xi:
                          e.dma_start(out=xb[xi % 2][:],
                                      in_=x_in[512 * sb + 128 * tt:
                                               512 * sb + 128 * (tt + 1), :]))

        @pb.body("tensor")
        def _(c):
            if A3_PHASE < 2:
                return
            c.raw(lambda e: e.wait_ge(swt, 16 * 40))
            for sb in range(NSB):
                for j in range(NE):
                    gj = 8 * sb + j
                    c.wait("vector", "h", 4 * sb + 3)
                    if gj >= 2:
                        c.wait("scalar", "he", gj - 2)

                    def trs(e, j=j):
                        for tt in range(4):
                            mm = e.matmul(
                                ps_tr[j % 2][:, 128 * tt:128 * (tt + 1)],
                                hb[tt][:, 128 * j:128 * (j + 1)],
                                t["ident"][:], start=True, stop=True)
                        return mm
                    c.op(trs)
                    c.mark("tp", gj)
                c.wait("scalar", "he", 8 * sb + 7)
                if A3_PHASE < 3:
                    continue
                for u in range(40):
                    gu = 56 * sb + u
                    pos = NU * sb + u
                    if pos >= 2:
                        psb, puu = divmod(pos - 2, NU)
                        en, key = pa_user(psb, puu)
                        c.wait(en, *key)
                    if u < 16:
                        hh = u

                        def qmm(e, hh=hh, gu=gu):
                            for e_ in range(NE):
                                mm = e.matmul(
                                    ps_a[gu % 2][:],
                                    WqD[e_][:, 128 * hh:128 * (hh + 1)],
                                    hT[e_][:], start=(e_ == 0),
                                    stop=(e_ == NE - 1))
                            return mm
                        c.op(qmm)
                    elif u < 32:
                        hh = u - 16

                        def kmm(e, hh=hh, gu=gu):
                            for e_ in range(NE):
                                mm = e.matmul(
                                    ps_a[gu % 2][:],
                                    WkD[e_][:, 128 * hh:128 * (hh + 1)],
                                    hT[e_][:], start=(e_ == 0),
                                    stop=(e_ == NE - 1))
                            return mm
                        c.op(kmm)
                    else:
                        hp = u - 32

                        def vmm(e, hp=hp, gu=gu):
                            for e_ in range(NE):
                                mm = e.matmul(
                                    ps_a[gu % 2][:],
                                    WvT[e_][:, 128 * hp:128 * (hp + 1)],
                                    hT[e_][:], start=(e_ == 0),
                                    stop=(e_ == NE - 1))
                            return mm
                        c.op(vmm)
                    c.mark("pm", gu)
                if A3_PHASE < 4:
                    continue
                c.wait("scalar", "pe", 56 * sb + 31)
                for gb in range(16):
                    gg = 16 * sb + gb
                    if gg >= 2:
                        c.wait("scalar", "ex", gg - 2)

                    def gmm(e, sb=sb, gb=gb, gg=gg):
                        for g2 in range(4):
                            gloc = 4 * gb + g2
                            mm = e.matmul(
                                ps_g[gg % 2][:, 128 * g2:128 * (g2 + 1)],
                                KD[:, 128 * gloc:128 * (gloc + 1)],
                                QD[:, 128 * gloc:128 * (gloc + 1)],
                                start=True, stop=True)
                        return mm
                    c.op(gmm)
                    c.mark("gm", gg)
                if A3_PHASE < 5:
                    continue
                c.wait("vector", "ve", 56 * sb + 39)
                for vb in range(16):
                    u = 40 + vb
                    gu = 56 * sb + u
                    pu = gu - 2
                    psb, puu = divmod(pu, 56)
                    en, key = pa_user(psb, puu)
                    c.wait(en, *key)

                    def v4mm(e, sb=sb, vb=vb, gu=gu):
                        for g2 in range(4):
                            gloc = 4 * vb + g2
                            for par in range(2):
                                lhs = bass.AP(VP, 64 * par * 4096 + 64 * gloc,
                                              [[4096, 64], [1, 64]])
                                mm = e.matmul(
                                    ps_a[gu % 2][64 * par:64 * (par + 1),
                                                 64 * g2:64 * (g2 + 1)],
                                    lhs, t["iic"][64 * par:64 * (par + 1), :],
                                    start=True, stop=True,
                                    tile_position=(64 * par, 64 * par))
                        return mm
                    c.op(v4mm)
                    c.mark("pm", gu)
                if A3_PHASE < 6:
                    continue
                for ab in range(16):
                    ga_ = 16 * sb + ab
                    gb = ab // 1
                    if ga_ >= 2 and A3_PHASE >= 8:
                        c.wait("scalar", "nm", ga_ - 2)
                    c.wait("vector", "mk", 16 * sb + ab)
                    c.wait("vector", "v4e", 56 * sb + 40 + ab)

                    def avmm(e, sb=sb, ab=ab, ga_=ga_):
                        for g2 in range(4):
                            mm = e.matmul(
                                ps_av[ga_ % 2][:, 65 * g2:65 * (g2 + 1)],
                                eS[ab][:, 128 * g2:128 * (g2 + 1)],
                                V4[ab][:, 65 * g2:65 * g2 + 65],
                                start=True, stop=True)
                        return mm
                    c.op(avmm)
                    c.mark("av", ga_)

        @pb.body("scalar")
        def _(c):
            for sb in range(NSB):
                for tt in range(4):
                    xi = 4 * sb + tt
                    c.raw(lambda e, xi=xi: e.wait_ge(sx[xi % 2],
                                                     16 * (xi // 2 + 1)))
                    if xi >= 2:
                        c.wait("scalar", "sq", xi - 2)
                        c.wait("vector", "v2", xi - 2)
                    c.op(lambda e, xi=xi: e.activation(
                        sqd[xi % 2][:], xb[xi % 2][:], AF.Square, bias=0.0,
                        scale=1.0, accum_out=st_[xi % 2][:, 1:2]))
                    c.mark("sq", xi)
                    c.wait("vector", "v2", xi)
                    c.op(lambda e, xi=xi: e.activation(
                        st_[xi % 2][:, 6:7], st_[xi % 2][:, 5:6], AF.Ln,
                        bias=0.0, scale=1.0))
                    c.wait_n("scalar", c.n)
                    c.op(lambda e, xi=xi: e.activation(
                        st_[xi % 2][:, 7:8], st_[xi % 2][:, 6:7], AF.Exp,
                        bias=0.0, scale=-0.5))
                    c.mark("rstd", xi)
                    c.wait("vector", "v3", xi)
                    if xi >= 2:
                        c.wait("vector", "h", xi - 2)
                    c.op(lambda e, xi=xi: e.activation(
                        lnx[xi % 2][:], xb[xi % 2][:], AF.Identity,
                        bias=st_[xi % 2][:, 9:10], scale=st_[xi % 2][:, 7:8]))
                    c.mark("lnx", xi)
                if A3_PHASE < 2:
                    continue
                for j in range(NE):
                    gj = 8 * sb + j
                    c.wait("tensor", "tp", gj)
                    c.op(lambda e, j=j: e.copy(hT[j][:], ps_tr[j % 2][:]))
                    c.mark("he", gj)
                if A3_PHASE < 3:
                    continue
                for u in range(32):
                    gu = 56 * sb + u
                    c.wait("tensor", "pm", gu)
                    if u < 16:
                        hh, slab, bia = u, QD, t["bqd"]
                    else:
                        hh, slab, bia = u - 16, KD, t["bkd"]
                    off = (hh % 2) * 64 + (hh // 2) * 8

                    def pevac(e, slab=slab, off=off, bia=bia, hh=hh, gu=gu):
                        dst = bass.AP(slab, off,
                                      [[16 * 512, 128], [128, 64], [1, 8]])
                        return e.activation(dst, ps_a[gu % 2][:], AF.Identity,
                                            bias=bia[:, hh:hh + 1], scale=1.0)
                    c.op(pevac)
                    c.mark("pe", gu)
                if A3_PHASE < 4:
                    continue
                for gb in range(16):
                    gg = 16 * sb + gb
                    c.wait("tensor", "gm", gg)
                    if gg >= 16 and A3_PHASE >= 6:
                        c.wait("tensor", "av", gg - 16)
                    c.op(lambda e, gb=gb, gg=gg: e.activation(
                        eS[gb][:], ps_g[gg % 2][:], AF.Exp, bias=0.0,
                        scale=0.0625))
                    c.mark("ex", gg)
                if A3_PHASE < 8:
                    continue
                for ab in range(16):
                    ga_ = 16 * sb + ab
                    st5 = ab // 2
                    sti = 8 * sb + st5
                    c.wait("tensor", "av", ga_)
                    c.wait("vector", "rc", ga_)
                    if ab % 2 == 0 and sti >= 2 and A3_PHASE >= 9:
                        c.raw(lambda e, sti=sti: e.wait_ge(
                            so5[sti % 2], 16 * (sti // 2)))
                    for g2 in range(4):
                        gs = (ab % 2) * 4 + g2
                        c.op(lambda e, ab=ab, g2=g2, ga_=ga_, sti=sti, gs=gs:
                             e.activation(
                                 o5s[sti % 2][:, 64 * gs:64 * (gs + 1)],
                                 ps_av[ga_ % 2][:, 65 * g2:65 * g2 + 64],
                                 AF.Identity, bias=0.0,
                                 scale=rec[:, 4 * (ab % 4) + g2:
                                           4 * (ab % 4) + g2 + 1]))
                    c.mark("nm", ga_)

        @pb.body("vector")
        def _(c):
            inv = 1.0 / E
            for i in range(16):
                c.op(lambda e, i=i: e.memset(
                    bass.AP(V4[i], 64, [[260, 128], [65, 4]]), 1.0))
            c.mark("vinit")
            for sb in range(NSB):
                for tt in range(4):
                    xi = 4 * sb + tt
                    sl = st_[xi % 2]
                    c.raw(lambda e, xi=xi: e.wait_ge(sx[xi % 2],
                                                     16 * (xi // 2 + 1)))
                    if xi >= 2:
                        c.wait("scalar", "lnx", xi - 2)
                    c.op(lambda e, xi=xi: e.reduce_sum(
                        st_[xi % 2][:, 0:1], xb[xi % 2][:], axis=AX))
                    c.mark("v1", xi)
                    c.wait("scalar", "sq", xi)
                    c.wait_n("vector", c.n)
                    c.op(lambda e, sl=sl: e.tensor_scalar(
                        sl[:, 2:3], sl[:, 0:1], inv, None, ALU.mult))
                    c.op(lambda e, sl=sl: e.tensor_scalar(
                        sl[:, 3:4], sl[:, 1:2], inv, None, ALU.mult))
                    c.wait_n("vector", c.n)
                    c.op(lambda e, sl=sl: e.tensor_mul(sl[:, 4:5], sl[:, 2:3],
                                                       sl[:, 2:3]))
                    c.wait_n("vector", c.n)
                    c.op(lambda e, sl=sl: e.tensor_sub(sl[:, 5:6], sl[:, 3:4],
                                                       sl[:, 4:5]))
                    c.wait_n("vector", c.n)
                    c.op(lambda e, sl=sl: e.tensor_scalar(
                        sl[:, 5:6], sl[:, 5:6], 1e-5, None, ALU.add))
                    c.mark("v2", xi)
                    c.wait("scalar", "rstd", xi)
                    c.op(lambda e, sl=sl: e.tensor_mul(sl[:, 8:9], sl[:, 2:3],
                                                       sl[:, 7:8]))
                    c.wait_n("vector", c.n)
                    c.op(lambda e, sl=sl: e.tensor_scalar(
                        sl[:, 9:10], sl[:, 8:9], -1.0, None, ALU.mult))
                    c.mark("v3", xi)
                    c.wait("scalar", "lnx", xi)
                    c.op(lambda e, xi=xi: e.tensor_mul(
                        lnx[xi % 2][:], lnx[xi % 2][:], t["ca1_rep"][:]))
                    c.wait_n("vector", c.n)
                    c.op(lambda e, xi=xi, tt=tt: e.tensor_add(
                        hb[tt][:], lnx[xi % 2][:], t["sa_rep"][:]))
                    c.mark("h", xi)
                if A3_PHASE < 3:
                    continue
                for hp in range(8):
                    gu = 56 * sb + 32 + hp
                    c.wait("tensor", "pm", gu)

                    def vevac(e, hp=hp, gu=gu):
                        dst = bass.AP(VP, 8 * hp,
                                      [[8 * 512, 128], [64, 64], [1, 8]])
                        return e.tensor_scalar(dst, ps_a[gu % 2][:],
                                               t["bvc"][:, hp:hp + 1], None,
                                               ALU.add)
                    c.op(vevac)
                    c.mark("ve", gu)
                if A3_PHASE < 4:
                    continue
                for gb in range(16):
                    gg = 16 * sb + gb
                    c.wait("scalar", "ex", gg)
                    c.op(lambda e, gb=gb: e.tensor_mul(eS[gb][:], eS[gb][:],
                                                       t["bdm"][:]))
                    c.mark("mk", gg)
                if A3_PHASE < 5:
                    continue
                for vb in range(16):
                    gu = 56 * sb + 40 + vb
                    c.wait("tensor", "pm", gu)
                    if 16 * sb + vb >= 16 and A3_PHASE >= 6:
                        c.wait("tensor", "av", 16 * sb + vb - 16)

                    def v4e(e, vb=vb, gu=gu):
                        src = bass.AP(ps_a[gu % 2], 0,
                                      [[512, 128], [64, 4], [1, 64]])
                        dst = bass.AP(V4[vb], 0, [[260, 128], [65, 4], [1, 64]])
                        return e.tensor_copy(dst, src)
                    c.op(v4e)
                    c.mark("v4e", gu)
                if A3_PHASE < 7:
                    continue
                for ab in range(16):
                    ga_ = 16 * sb + ab
                    c.wait("tensor", "av", ga_)
                    if ga_ >= 4 and A3_PHASE >= 8:
                        c.wait("scalar", "nm", ga_ - 4)

                    def rcp(e, ab=ab, ga_=ga_):
                        src = bass.AP(ps_av[ga_ % 2], 64, [[260, 128], [65, 4]])
                        return e.reciprocal(rec[:, 4 * (ab % 4):
                                                4 * (ab % 4) + 4], src)
                    c.op(rcp)
                    c.mark("rc", ga_)

        @pb.body("gpsimd")
        def _(c):
            if A3_PHASE < 9:
                return
            for sb in range(NSB):
                for st5 in range(8):
                    sti = 8 * sb + st5
                    c.wait("scalar", "nm", 16 * sb + 2 * st5 + 1)

                    def o5dma(e, sb=sb, st5=st5, sti=sti):
                        dst = bass.AP(o5_scr, (64 * sb + 8 * st5) * 8192,
                                      [[64, 128], [8192, 8], [1, 64]])
                        src = bass.AP(o5s[sti % 2], 0,
                                      [[512, 128], [64, 8], [1, 64]])
                        return e.dma_start(out=dst, in_=src)
                    c.dma(so5[sti % 2], o5dma)
            c.raw(lambda e: e.wait_ge(so5[0], 16 * 4 * NSB))
            c.raw(lambda e: e.wait_ge(so5[1], 16 * 4 * NSB))

        pb.finish()


def emit_b6op(nc, t, S, x_in, o5_scr, WoTga, gabo_bf, xnew_scr):
    NT = S // 128
    with contextlib.ExitStack() as es:
        o2b = [es.enter_context(nc.sbuf_tensor(f"b6ob{i}", [128, E], BF16))
               for i in range(2)]
        o2T = [es.enter_context(nc.sbuf_tensor(f"b6oT{i}", [128, E], BF16))
               for i in range(2)]
        xt = [es.enter_context(nc.sbuf_tensor(f"b6x{i}", [128, E], F32))
              for i in range(2)]
        ps_tr = [es.enter_context(nc.psum_tensor(f"b6pt{i}", [128, 512], F32))
                 for i in range(2)]
        ps_o = [es.enter_context(nc.psum_tensor(f"b6po{i}", [128, 512], F32))
                for i in range(2)]

        pb = PB(nc, es, "B6", ["sync", "gpsimd", "vector", "scalar", "tensor"])
        ol = pb.dma_sem("ol", 2)
        xl = pb.dma_sem("xl", 2)
        xs = pb.dma_sem("xs", 2)

        R = S // 16

        @pb.body("sync")
        def _(c):
            for k in range(NT):
                if k >= 2:
                    c.wait("tensor", "tp", 2 * (k - 2) + 1)
                nsub = max(1, 128 // R)
                psz = 128 // nsub
                for a in range(nsub):
                    hq0 = (128 * k + a * psz) // R
                    n0 = (hq0 % 2) * 64 + (hq0 // 2) * 8
                    rb = (128 * k + a * psz) % R

                    def o2dma(e, k=k, a=a, psz=psz, n0=n0, rb=rb):
                        src = bass.AP(o5_scr, 16384 * rb + 64 * n0,
                                      [[16384, psz], [8192, 2], [64, 8],
                                       [1, 64]])
                        dst = bass.AP(o2b[k % 2], a * psz * E,
                                      [[E, psz], [512, 2], [64, 8], [1, 64]])
                        return e.dma_start(out=dst, in_=src)
                    c.dma(ol[k % 2], o2dma)
                if k >= 2:
                    c.wait("vector", "rs", k - 2)
                    c.raw(lambda e, k=k: e.wait_ge(xs[k % 2], 16 * (k // 2)))
                c.dma(xl[k % 2], lambda e, k=k: e.dma_start(
                    out=xt[k % 2][:],
                    in_=x_in[128 * k:128 * (k + 1), :]))

        @pb.body("tensor")
        def _(c):
            nsub = max(1, 128 // R)
            for k in range(NT):
                c.raw(lambda e, k=k, nsub=nsub: e.wait_ge(
                    ol[k % 2], 16 * nsub * (k // 2 + 1)))
                for b in range(2):
                    gb = 2 * k + b
                    if gb >= 2:
                        c.wait("scalar", "oe", gb - 2)

                    def trs(e, k=k, b=b):
                        for jj in range(4):
                            mm = e.matmul(
                                ps_tr[b % 2][:, 128 * jj:128 * (jj + 1)],
                                o2b[k % 2][:, 512 * b + 128 * jj:
                                            512 * b + 128 * (jj + 1)],
                                t["ident"][:], start=True, stop=True)
                        return mm
                    c.op(trs)
                    c.mark("tp", gb)
                c.wait("scalar", "oe", 2 * k + 1)
                if k >= 1:
                    c.wait("vector", "rs", k - 1)
                for fh in range(2):
                    def opmm(e, k=k, fh=fh):
                        for pp in range(NE):
                            e.matmul(ps_o[fh][:],
                                     o2T[k % 2][:, 128 * pp:128 * (pp + 1)],
                                     WoTga[pp][:, 512 * fh:512 * (fh + 1)],
                                     start=(pp == 0), stop=False)
                        return e.matmul(ps_o[fh][:], t["onesb"][0:1, :],
                                        gabo_bf[0:1, 512 * fh:512 * (fh + 1)],
                                        start=False, stop=True)
                    c.op(opmm)
                c.mark("op", k)

        @pb.body("scalar")
        def _(c):
            for k in range(NT):
                for b in range(2):
                    gb = 2 * k + b
                    c.wait("tensor", "tp", gb)
                    if k >= 2:
                        c.wait("tensor", "op", k - 2)
                    c.op(lambda e, k=k, b=b: e.copy(
                        o2T[k % 2][:, 512 * b:512 * (b + 1)],
                        ps_tr[b % 2][:]))
                    c.mark("oe", gb)

        @pb.body("vector")
        def _(c):
            for k in range(NT):
                c.wait("tensor", "op", k)
                c.raw(lambda e, k=k: e.wait_ge(xl[k % 2], 16 * (k // 2 + 1)))
                c.op(lambda e, k=k: e.tensor_add(
                    xt[k % 2][:, 0:512], ps_o[0][:], xt[k % 2][:, 0:512]))
                c.op(lambda e, k=k: e.tensor_add(
                    xt[k % 2][:, 512:1024], ps_o[1][:],
                    xt[k % 2][:, 512:1024]))
                c.mark("rs", k)

        @pb.body("gpsimd")
        def _(c):
            for k in range(NT):
                c.wait("vector", "rs", k)
                c.dma(xs[k % 2], lambda e, k=k: e.dma_start(
                    out=xnew_scr[128 * k:128 * (k + 1), :],
                    in_=xt[k % 2][:]))
            c.raw(lambda e: e.wait_ge(xs[0], 16 * NT // 2))
            c.raw(lambda e: e.wait_ge(xs[1], 16 * NT // 2))

        pb.finish()


def emit_b5(nc, t, wT_scr, modrow_scr, bo_in, WoTga, gabo_bf):
    with contextlib.ExitStack() as es:
        wo_t = [es.enter_context(nc.sbuf_tensor(f"b5w{j}", [128, E], BF16))
                for j in range(NE)]
        garep = es.enter_context(nc.sbuf_tensor("b5gar", [128, E], F32))
        vrow = es.enter_context(nc.sbuf_tensor("b5v", [1, E], F32))
        grow = es.enter_context(nc.sbuf_tensor("b5g", [1, E], F32))
        srow = es.enter_context(nc.sbuf_tensor("b5s", [1, E], F32))
        borow = es.enter_context(nc.sbuf_tensor("b5bo", [1, E], F32))
        ps_bc = es.enter_context(nc.psum_tensor("b5p", [128, 512], F32))

        q = Seq(nc, "B5")
        for j in range(NE):
            q.step("sync", lambda e, j=j: e.dma_start(
                out=wo_t[j][:], in_=wT_scr["wo"][128 * j:128 * (j + 1), :]),
                dma=True)
        q.step("sync", lambda e: e.dma_start(
            out=vrow[0:1, :], in_=modrow_scr[0:1, 2 * E:3 * E]), dma=True)
        q.step("sync", lambda e: e.dma_start(out=borow[0:1, :], in_=bo_in),
               dma=True)
        q.step("vector", lambda e: e.tensor_scalar(srow[0:1, :], vrow[0:1, :],
                                                   0.0, None, ALU.is_ge))
        q.step("vector", lambda e: e.tensor_scalar(srow[0:1, :], srow[0:1, :],
                                                   2.0, -1.0, ALU.mult,
                                                   ALU.add))
        q.step("vector", lambda e: e.tensor_scalar(grow[0:1, :], vrow[0:1, :],
                                                   -1.0, None, ALU.mult))
        q.step("vector", lambda e: e.tensor_max(grow[0:1, :], grow[0:1, :],
                                                vrow[0:1, :]))
        q.step("vector", lambda e: e.tensor_scalar(grow[0:1, :], grow[0:1, :],
                                                   0.25, None, ALU.max))
        q.step("vector", lambda e: e.tensor_mul(grow[0:1, :], grow[0:1, :],
                                                srow[0:1, :]))
        for half in range(2):
            q.step("tensor", lambda e, h=half: e.matmul(
                ps_bc[:, 0:512], t["onesf"][0:1, :],
                grow[0:1, 512 * h:512 * (h + 1)], start=True, stop=True))
            q.step("vector", lambda e, h=half: e.tensor_copy(
                garep[:, 512 * h:512 * (h + 1)], ps_bc[:, 0:512]))
        q.step("vector", lambda e: e.tensor_mul(vrow[0:1, :], borow[0:1, :],
                                                grow[0:1, :]))
        q.step("vector", lambda e: e.tensor_copy(gabo_bf[0:1, :],
                                                 vrow[0:1, :]))
        for j in range(NE):
            q.step("vector", lambda e, j=j: e.tensor_mul(
                WoTga[j][:], wo_t[j][:], garep[:]))
        q.emit(es)


def emit_b7b8(nc, t, S, packed_in, fbias_in, modrow_scr, xnew_scr, out_hbm):
    NSB = S // 512
    with contextlib.ExitStack() as es:
        wgbT = [es.enter_context(nc.sbuf_tensor(f"b8w{j}", [128, E], BF16))
                for j in range(NE)]
        cb1_rep = es.enter_context(nc.sbuf_tensor("b8cb1", [128, E], F32))
        sb_rep = es.enter_context(nc.sbuf_tensor("b8sbr", [128, E], F32))
        gbias_rep = es.enter_context(nc.sbuf_tensor("b8gbi", [128, E], F32))
        gb_rep = es.enter_context(nc.sbuf_tensor("b8gbr", [128, E], F32))

        with contextlib.ExitStack() as es7:
            pk = [es7.enter_context(nc.sbuf_tensor(f"b7p{i}", [128, E], I32))
                  for i in range(2)]
            dec = es7.enter_context(nc.sbuf_tensor("b7d", [128, E], I32))
            wdec = [es7.enter_context(nc.sbuf_tensor(f"b7wd{i}", [128, E], BF16))
                    for i in range(8)]
            vrow = es7.enter_context(nc.sbuf_tensor("b7v", [1, E], F32))
            grow = es7.enter_context(nc.sbuf_tensor("b7g", [1, E], F32))
            srow = es7.enter_context(nc.sbuf_tensor("b7s", [1, E], F32))
            frow = es7.enter_context(nc.sbuf_tensor("b7f", [1, E], F32))
            ps_bc = es7.enter_context(nc.psum_tensor("b7pb", [128, 512], F32))
            ps_tr = [es7.enter_context(nc.psum_tensor(f"b7pt{i}", [128, E],
                                                      F32)) for i in range(2)]
            q = Seq(nc, "B7")
            for pt in range(2):
                q.step("sync", lambda e, pt=pt: e.dma_start(
                    out=pk[pt][:],
                    in_=packed_in[128 * pt:128 * (pt + 1), :]), dma=True)
            for pt in range(2):
                for b in range(4):
                    oc = 2 * b + pt
                    q.step("vector", lambda e, pt=pt, b=b: e.tensor_scalar(
                        dec[:], pk[pt][:], int(2 * b), None,
                        ALU.logical_shift_right))
                    q.step("vector", lambda e: e.tensor_scalar(
                        dec[:], dec[:], 3, None, ALU.bitwise_and))
                    q.step("vector", lambda e, oc=oc: e.tensor_scalar(
                        wdec[oc][:], dec[:], -1.0, None, ALU.add))
            q.step("sync", lambda e: e.dma_start(
                out=vrow[0:1, :], in_=modrow_scr[0:1, 5 * E:6 * E]), dma=True)
            q.step("sync", lambda e: e.dma_start(out=frow[0:1, :],
                                                 in_=fbias_in), dma=True)
            q.step("vector", lambda e: e.tensor_scalar(srow[0:1, :],
                                                       vrow[0:1, :], 0.0,
                                                       None, ALU.is_ge))
            q.step("vector", lambda e: e.tensor_scalar(srow[0:1, :],
                                                       srow[0:1, :], 2.0,
                                                       -1.0, ALU.mult,
                                                       ALU.add))
            q.step("vector", lambda e: e.tensor_scalar(grow[0:1, :],
                                                       vrow[0:1, :], -1.0,
                                                       None, ALU.mult))
            q.step("vector", lambda e: e.tensor_max(grow[0:1, :],
                                                    grow[0:1, :],
                                                    vrow[0:1, :]))
            q.step("vector", lambda e: e.tensor_scalar(grow[0:1, :],
                                                       grow[0:1, :], 0.25,
                                                       None, ALU.max))
            q.step("vector", lambda e: e.tensor_mul(grow[0:1, :], grow[0:1, :],
                                                    srow[0:1, :]))
            q.step("vector", lambda e: e.tensor_mul(frow[0:1, :], frow[0:1, :],
                                                    grow[0:1, :]))
            for half in range(2):
                q.step("tensor", lambda e, h=half: e.matmul(
                    ps_bc[:, 0:512], t["onesf"][0:1, :],
                    grow[0:1, 512 * h:512 * (h + 1)], start=True, stop=True))
                q.step("vector", lambda e, h=half: e.tensor_copy(
                    gb_rep[:, 512 * h:512 * (h + 1)], ps_bc[:, 0:512]))
            for half in range(2):
                q.step("tensor", lambda e, h=half: e.matmul(
                    ps_bc[:, 0:512], t["onesf"][0:1, :],
                    frow[0:1, 512 * h:512 * (h + 1)], start=True, stop=True))
                q.step("vector", lambda e, h=half: e.tensor_copy(
                    gbias_rep[:, 512 * h:512 * (h + 1)], ps_bc[:, 0:512]))
            q.step("sync", lambda e: e.dma_start(
                out=vrow[0:1, :], in_=modrow_scr[0:1, 4 * E:5 * E]), dma=True)
            q.step("vector", lambda e: e.tensor_scalar(vrow[0:1, :],
                                                       vrow[0:1, :], 1.0,
                                                       None, ALU.add))
            for half in range(2):
                q.step("tensor", lambda e, h=half: e.matmul(
                    ps_bc[:, 0:512], t["onesf"][0:1, :],
                    vrow[0:1, 512 * h:512 * (h + 1)], start=True, stop=True))
                q.step("vector", lambda e, h=half: e.tensor_copy(
                    cb1_rep[:, 512 * h:512 * (h + 1)], ps_bc[:, 0:512]))
            q.step("sync", lambda e: e.dma_start(
                out=vrow[0:1, :], in_=modrow_scr[0:1, 3 * E:4 * E]), dma=True)
            for half in range(2):
                q.step("tensor", lambda e, h=half: e.matmul(
                    ps_bc[:, 0:512], t["onesf"][0:1, :],
                    vrow[0:1, 512 * h:512 * (h + 1)], start=True, stop=True))
                q.step("vector", lambda e, h=half: e.tensor_copy(
                    sb_rep[:, 512 * h:512 * (h + 1)], ps_bc[:, 0:512]))
            for j in range(NE):
                def trs(e, j=j):
                    for tt in range(8):
                        mm = e.matmul(
                            ps_tr[j % 2][:, 128 * tt:128 * (tt + 1)],
                            wdec[tt][:, 128 * j:128 * (j + 1)], t["ident"][:],
                            start=True, stop=True)
                    return mm
                q.step("tensor", trs)
                q.step("scalar", lambda e, j=j: e.copy(wgbT[j][:],
                                                       ps_tr[j % 2][:]))
                q.step("vector", lambda e, j=j: e.tensor_mul(
                    wgbT[j][:], wgbT[j][:], gb_rep[:]))
            q.emit(es7)

        with contextlib.ExitStack() as es8:
            xn = [es8.enter_context(nc.sbuf_tensor(f"b8x{i}", [128, E], F32))
                  for i in range(2)]
            lnx = [es8.enter_context(nc.sbuf_tensor(f"b8l{i}", [128, E], F32))
                   for i in range(2)]
            h2t = [es8.enter_context(nc.sbuf_tensor(f"b8h{i}", [128, E], F32))
                   for i in range(2)]
            pre = [es8.enter_context(nc.sbuf_tensor(f"b8pr{i}", [128, E], F32))
                   for i in range(4)]
            tq = [es8.enter_context(nc.sbuf_tensor(f"b8t{i}", [128, E], BF16))
                  for i in range(4)]
            tT = [es8.enter_context(nc.sbuf_tensor(f"b8tt{j}", [128, 512],
                                                   BF16)) for j in range(NE)]
            ytT = [es8.enter_context(nc.sbuf_tensor(f"b8y{j}", [128, 512],
                                                    BF16)) for j in range(NE)]
            ut = [es8.enter_context(nc.sbuf_tensor(f"b8u{i}", [128, E], F32))
                  for i in range(2)]
            ot_t = [es8.enter_context(nc.sbuf_tensor(f"b8ot{i}", [128, E],
                                                     F32)) for i in range(2)]
            sqd = [es8.enter_context(nc.sbuf_tensor(f"b8sq{i}", [128, E],
                                                    BF16)) for i in range(2)]
            w0 = es8.enter_context(nc.sbuf_tensor("b8z0", [128, E], F32))
            w1 = es8.enter_context(nc.sbuf_tensor("b8z1", [128, E], I32))
            w2 = es8.enter_context(nc.sbuf_tensor("b8z2", [128, E], F32))
            w3 = es8.enter_context(nc.sbuf_tensor("b8z3", [128, E], F32))
            w4 = es8.enter_context(nc.sbuf_tensor("b8z4", [128, E], F32))
            st_ = [es8.enter_context(nc.sbuf_tensor(f"b8st{i}", [128, 16],
                                                    F32)) for i in range(2)]
            fsc = es8.enter_context(nc.sbuf_tensor("b8fs", [128, 4], F32))
            ps_tr2 = [es8.enter_context(nc.psum_tensor(f"b8p2{i}", [128, 512],
                                                       F32)) for i in range(2)]
            ps_mm = [es8.enter_context(nc.psum_tensor(f"b8pm{i}", [128, 512],
                                                      F32)) for i in range(2)]
            ps_tok = [es8.enter_context(nc.psum_tensor(f"b8pk{i}", [128, E],
                                                       F32)) for i in range(2)]

            pb = PB(nc, es8, "B8",
                    ["sync", "gpsimd", "vector", "scalar", "tensor"])
            sx = pb.dma_sem("sx", 2)
            osym = pb.dma_sem("os", 2)

            @pb.body("sync")
            def _(c):
                for xi in range(4 * NSB):
                    if xi >= 2:
                        c.wait("vector", "pre", xi - 2)
                    c.dma(sx[xi % 2], lambda e, xi=xi: e.dma_start(
                        out=xn[xi % 2][:],
                        in_=xnew_scr[128 * xi:128 * (xi + 1), :]))

            @pb.body("scalar")
            def _(c):
                for sb in range(NSB):
                    for tt in range(4):
                        xi = 4 * sb + tt
                        c.raw(lambda e, xi=xi: e.wait_ge(sx[xi % 2],
                                                         16 * (xi // 2 + 1)))
                        if xi >= 2:
                            c.wait("scalar", "sq", xi - 2)
                            c.wait("vector", "v2", xi - 2)
                        c.op(lambda e, xi=xi: e.activation(
                            sqd[xi % 2][:], xn[xi % 2][:], AF.Square,
                            bias=0.0, scale=1.0,
                            accum_out=st_[xi % 2][:, 1:2]))
                        c.mark("sq", xi)
                        c.wait("vector", "v2", xi)
                        c.op(lambda e, xi=xi: e.activation(
                            st_[xi % 2][:, 6:7], st_[xi % 2][:, 5:6], AF.Ln,
                            bias=0.0, scale=1.0))
                        c.wait_n("scalar", c.n)
                        c.op(lambda e, xi=xi: e.activation(
                            st_[xi % 2][:, 7:8], st_[xi % 2][:, 6:7], AF.Exp,
                            bias=0.0, scale=-0.5))
                        c.mark("rstd", xi)
                        c.wait("vector", "v3", xi)
                        if xi >= 2:
                            c.wait("vector", "h2", xi - 2)
                        c.op(lambda e, xi=xi: e.activation(
                            lnx[xi % 2][:], xn[xi % 2][:], AF.Identity,
                            bias=st_[xi % 2][:, 9:10],
                            scale=st_[xi % 2][:, 7:8]))
                        c.mark("lnx", xi)
                    for j in range(NE):
                        gj = 8 * sb + j
                        c.wait("tensor", "tp", gj)
                        c.op(lambda e, j=j: e.copy(tT[j][:], ps_tr2[j % 2][:]))
                        c.mark("he", gj)
                    for f in range(NE):
                        gf = 8 * sb + f
                        c.wait("tensor", "fm", gf)
                        if sb >= 1:
                            c.wait("tensor", "bt", 4 * sb - 1)
                        c.op(lambda e, f=f: e.copy(ytT[f][:], ps_mm[f % 2][:]))
                        c.mark("fe", gf)
                    for tt in range(4):
                        xi = 4 * sb + tt
                        c.wait("tensor", "bt", xi)
                        c.wait("vector", "fsc", xi)
                        if xi >= 2:
                            c.wait("vector", "fin", xi - 2)
                        c.op(lambda e, xi=xi, tt=tt: e.activation(
                            ut[xi % 2][:], ps_tok[xi % 2][:], AF.Identity,
                            bias=0.0, scale=fsc[:, tt:tt + 1]))
                        c.mark("ue", xi)

            @pb.body("vector")
            def _(c):
                inv = 1.0 / E
                for sb in range(NSB):
                    for tt in range(4):
                        xi = 4 * sb + tt
                        sl = st_[xi % 2]
                        c.raw(lambda e, xi=xi: e.wait_ge(sx[xi % 2],
                                                         16 * (xi // 2 + 1)))
                        if xi >= 2:
                            c.wait("scalar", "lnx", xi - 2)
                        c.op(lambda e, xi=xi: e.reduce_sum(
                            st_[xi % 2][:, 0:1], xn[xi % 2][:], axis=AX))
                        c.mark("v1", xi)
                        c.wait("scalar", "sq", xi)
                        c.wait_n("vector", c.n)
                        c.op(lambda e, sl=sl: e.tensor_scalar(
                            sl[:, 2:3], sl[:, 0:1], inv, None, ALU.mult))
                        c.op(lambda e, sl=sl: e.tensor_scalar(
                            sl[:, 3:4], sl[:, 1:2], inv, None, ALU.mult))
                        c.wait_n("vector", c.n)
                        c.op(lambda e, sl=sl: e.tensor_mul(
                            sl[:, 4:5], sl[:, 2:3], sl[:, 2:3]))
                        c.wait_n("vector", c.n)
                        c.op(lambda e, sl=sl: e.tensor_sub(
                            sl[:, 5:6], sl[:, 3:4], sl[:, 4:5]))
                        c.wait_n("vector", c.n)
                        c.op(lambda e, sl=sl: e.tensor_scalar(
                            sl[:, 5:6], sl[:, 5:6], 1e-5, None, ALU.add))
                        c.mark("v2", xi)
                        c.wait("scalar", "rstd", xi)
                        c.op(lambda e, sl=sl: e.tensor_mul(
                            sl[:, 8:9], sl[:, 2:3], sl[:, 7:8]))
                        c.wait_n("vector", c.n)
                        c.op(lambda e, sl=sl: e.tensor_scalar(
                            sl[:, 9:10], sl[:, 8:9], -1.0, None, ALU.mult))
                        c.mark("v3", xi)
                        c.wait("scalar", "lnx", xi)
                        c.op(lambda e, xi=xi: e.tensor_mul(
                            h2t[xi % 2][:], lnx[xi % 2][:], cb1_rep[:]))
                        c.wait_n("vector", c.n)
                        c.op(lambda e, xi=xi: e.tensor_add(
                            h2t[xi % 2][:], h2t[xi % 2][:], sb_rep[:]))
                        c.mark("h2", xi)
                        c.wait_n("vector", c.n)
                        c.op(lambda e, xi=xi, sl=sl: e.reduce_max(
                            sl[:, 10:11], h2t[xi % 2][:], axis=AX,
                            apply_absolute_value=True))
                        c.wait_n("vector", c.n)
                        c.op(lambda e, sl=sl: e.tensor_scalar(
                            sl[:, 11:12], sl[:, 10:11], 1e-5, None, ALU.max))
                        c.wait_n("vector", c.n)
                        c.op(lambda e, sl=sl: e.reciprocal(
                            sl[:, 12:13], sl[:, 11:12]))
                        c.wait_n("vector", c.n)
                        c.op(lambda e, sl=sl: e.tensor_scalar(
                            sl[:, 13:14], sl[:, 12:13], 127.0, None,
                            ALU.mult))
                        c.op(lambda e, sl=sl: e.tensor_mul(
                            sl[:, 14:15], sl[:, 11:12], t["ws_rep"][:]))
                        c.wait_n("vector", c.n)
                        c.op(lambda e, sl=sl, tt=tt: e.tensor_scalar(
                            fsc[:, tt:tt + 1], sl[:, 14:15], 1.0 / 127.0,
                            None, ALU.mult))
                        c.mark("fsc", xi)
                        c.op(lambda e, xi=xi, sl=sl: e.tensor_scalar(
                            w0[:], h2t[xi % 2][:], sl[:, 13:14], None,
                            ALU.mult))
                        c.wait_n("vector", c.n)
                        c.op(lambda e: e.tensor_copy(w1[:], w0[:]))
                        c.wait_n("vector", c.n)
                        c.op(lambda e: e.tensor_copy(w2[:], w1[:]))
                        c.wait_n("vector", c.n)
                        c.op(lambda e: e.tensor_sub(w3[:], w0[:], w2[:]))
                        c.wait_n("vector", c.n)
                        c.op(lambda e: e.tensor_scalar(w4[:], w3[:], 0.5,
                                                       None, ALU.is_gt))
                        c.wait_n("vector", c.n)
                        c.op(lambda e: e.tensor_add(w2[:], w2[:], w4[:]))
                        c.wait_n("vector", c.n)
                        c.op(lambda e: e.tensor_scalar(w4[:], w3[:], -0.5,
                                                       None, ALU.is_lt))
                        c.wait_n("vector", c.n)
                        if sb >= 1:
                            c.wait("tensor", "tp", 8 * sb - 1)
                        c.op(lambda e, tt=tt: e.tensor_sub(tq[tt][:], w2[:],
                                                           w4[:]))
                        c.mark("tq", xi)
                        if sb >= 1:
                            c.wait("vector", "fin", 4 * (sb - 1) + tt)
                        c.op(lambda e, xi=xi, tt=tt: e.tensor_add(
                            pre[tt][:], xn[xi % 2][:], gbias_rep[:]))
                        c.mark("pre", xi)
                    for tt in range(4):
                        xi = 4 * sb + tt
                        c.wait("scalar", "ue", xi)
                        if xi >= 2:
                            c.raw(lambda e, xi=xi: e.wait_ge(
                                osym[xi % 2], 16 * (xi // 2)))
                        c.op(lambda e, xi=xi, tt=tt: e.tensor_add(
                            ot_t[xi % 2][:], pre[tt][:], ut[xi % 2][:]))
                        c.mark("fin", xi)

            @pb.body("tensor")
            def _(c):
                for sb in range(NSB):
                    c.wait("vector", "tq", 4 * sb + 3)
                    for j in range(NE):
                        gj = 8 * sb + j
                        if gj >= 2:
                            c.wait("scalar", "he", gj - 2)
                        def trs(e, j=j):
                            for tt in range(4):
                                mm = e.matmul(
                                    ps_tr2[j % 2][:, 128 * tt:128 * (tt + 1)],
                                    tq[tt][:, 128 * j:128 * (j + 1)],
                                    t["ident"][:], start=True, stop=True)
                            return mm
                        c.op(trs)
                        c.mark("tp", gj)
                    c.wait("scalar", "he", 8 * sb + 7)
                    for f in range(NE):
                        gf = 8 * sb + f
                        if gf >= 2:
                            c.wait("scalar", "fe", gf - 2)
                        def fmm(e, f=f):
                            for e_ in range(NE):
                                mm = e.matmul(
                                    ps_mm[f % 2][:],
                                    wgbT[e_][:, 128 * f:128 * (f + 1)],
                                    tT[e_][:], start=(e_ == 0),
                                    stop=(e_ == NE - 1))
                            return mm
                        c.op(fmm)
                        c.mark("fm", gf)
                    c.wait("scalar", "fe", 8 * sb + 7)
                    for tt in range(4):
                        xi = 4 * sb + tt
                        if xi >= 2:
                            c.wait("scalar", "ue", xi - 2)
                        def btr(e, tt=tt, xi=xi):
                            for f in range(NE):
                                mm = e.matmul(
                                    ps_tok[xi % 2][:, 128 * f:128 * (f + 1)],
                                    ytT[f][:, 128 * tt:128 * (tt + 1)],
                                    t["ident"][:], start=True, stop=True)
                            return mm
                        c.op(btr)
                        c.mark("bt", xi)

            @pb.body("gpsimd")
            def _(c):
                for xi in range(4 * NSB):
                    c.wait("vector", "fin", xi)
                    c.dma(osym[xi % 2], lambda e, xi=xi: e.dma_start(
                        out=out_hbm[128 * xi:128 * (xi + 1), :],
                        in_=ot_t[xi % 2][:]))
                c.raw(lambda e: e.wait_ge(osym[0], 16 * 2 * NSB))
                c.raw(lambda e: e.wait_ge(osym[1], 16 * 2 * NSB))

            pb.finish()


def _prep_core_inputs(b, x_full, Wq, bq, Wk, bk, Wv, bv, Wo, bo, packed, ws,
                      fbias, C, wmod):
    bf = ml_dtypes.bfloat16
    def dupb(v):
        m = v.reshape(H, 64).T
        return np.ascontiguousarray(np.vstack([m, m]))
    bvc = np.ascontiguousarray(
        bv.reshape(NE, 2, 64).transpose(1, 2, 0).reshape(128, NE))
    r = np.arange(128)[:, None]
    cc = np.arange(512)[None, :]
    bdm = ((r % 8) == (cc % 8)).astype(bf)
    iic = np.vstack([np.eye(64), np.eye(64)]).astype(bf)
    d = {
        "x": np.ascontiguousarray(x_full[b]),
        "wq": Wq, "wk": Wk, "wv": Wv, "wo": Wo,
        "bqd": dupb(bq), "bkd": dupb(bk), "bvc": bvc,
        "bdm": np.ascontiguousarray(bdm),
        "iic": np.ascontiguousarray(iic),
        "bo": np.ascontiguousarray(bo.reshape(1, E)),
        "packed": np.ascontiguousarray(packed.reshape(256, E)),
        "ws": np.ascontiguousarray(ws.reshape(1, 1)),
        "fbias": np.ascontiguousarray(fbias.reshape(1, E)),
        "ccol": np.ascontiguousarray(C[b].reshape(NE, 128).T),
        "wmod": wmod,
        "identc": np.eye(128, dtype=bf),
        "onesbc": np.ones((128, 128), dtype=bf),
        "onesfc": np.ones((128, 128), dtype=np.float32),
    }
    return d


def kernel(L0_attn_x, L0_attn_Wq, L0_attn_bq, L0_attn_Wk, L0_attn_bk,
           L0_attn_Wv, L0_attn_bv, L0_attn_Wo, L0_attn_bo,
           L0_ffn_packed_weight, L0_ffn_weight_scale, L0_ffn_bias,
           L0_c, L0_wmod):
    S = 4096
    x_full = np.asarray(L0_attn_x, dtype=np.float32)
    args = [np.ascontiguousarray(np.asarray(a, dtype=np.float32))
            for a in (L0_attn_Wq, L0_attn_bq, L0_attn_Wk, L0_attn_bk,
                      L0_attn_Wv, L0_attn_bv, L0_attn_Wo, L0_attn_bo)]
    Wq, bq, Wk, bk, Wv, bv, Wo, bo = args
    packed = np.ascontiguousarray(np.asarray(L0_ffn_packed_weight,
                                             dtype=np.int32))
    ws = np.asarray(L0_ffn_weight_scale, dtype=np.float32)
    fbias = np.asarray(L0_ffn_bias, dtype=np.float32)
    C = np.ascontiguousarray(np.asarray(L0_c, dtype=np.float32))
    wmod = np.ascontiguousarray(np.asarray(L0_wmod, dtype=np.float32))

    nc = build_program(S)
    in_maps = [_prep_core_inputs(b, x_full, Wq, bq, Wk, bk, Wv, bv, Wo, bo,
                                 packed, ws, fbias, C, wmod)
               for b in range(8)]
    res = run_bass_kernel_spmd(nc, in_maps, core_ids=list(range(8)))
    out = np.stack([np.asarray(res.results[b]["out"], dtype=np.float32)
                    for b in range(8)], axis=0)
    return out


# version 07 (final)
# speedup vs baseline: 1.1715x; 1.1715x over previous
import contextlib
import numpy as np
import ml_dtypes

import concourse.bass as bass
import concourse.mybir as mybir
from concourse.bass_utils import run_bass_kernel_spmd

F32 = mybir.dt.float32
BF16 = mybir.dt.bfloat16
I32 = mybir.dt.int32
AF = mybir.ActivationFunctionType
ALU = mybir.AluOpType
AX = mybir.AxisListType.X

E = 1024
H = 16
NE = E // 128

DEBUG_OUTPUTS = False
STOP_AFTER = None
A3_SB = None
A3_PHASE = 9


class Seq:

    def __init__(self, nc, name):
        self.nc = nc
        self.name = name
        self.steps = []
        self.extra = {}

    def step(self, eng, fn, dma=False):
        self.steps.append((eng, fn, 16 if dma else 1))
        return len(self.steps) - 1

    def after_counts(self, sid):
        cc = cd = 0
        for i in range(sid + 1):
            if self.steps[i][2] == 16:
                cd += 16
            else:
                cc += 1
        return cc, cd

    def emit(self, es):
        nc = self.nc
        semc = es.enter_context(nc.semaphore(f"{self.name}_sc"))
        semd = es.enter_context(nc.semaphore(f"{self.name}_sd"))
        prec, pred = [], []
        cc = cd = 0
        for (_, _, inc) in self.steps:
            prec.append(cc)
            pred.append(cd)
            if inc == 16:
                cd += 16
            else:
                cc += 1
        blk = es.enter_context(nc.Block(self.name))
        resolver = (semc, semd, prec, pred)
        for engname in ("sync", "gpsimd", "vector", "scalar", "tensor"):
            mysteps = [(i, f, inc) for i, (en, f, inc) in enumerate(self.steps)
                       if en == engname]
            if not mysteps and engname not in self.extra:
                continue

            def body(eng, mysteps=mysteps, engname=engname):
                if engname in self.extra:
                    self.extra[engname](eng, resolver)
                for i, f, inc in mysteps:
                    if prec[i] > 0:
                        eng.wait_ge(semc, prec[i])
                    if pred[i] > 0:
                        eng.wait_ge(semd, pred[i])
                    instr = f(eng)
                    instr.then_inc(semd if inc == 16 else semc, inc)
                if cc > 0:
                    eng.wait_ge(semc, cc)
                if cd > 0:
                    eng.wait_ge(semd, cd)
            getattr(blk, engname)(body)
        return semc, semd


class PB:

    def __init__(self, nc, es, name, engines):
        self.nc = nc
        self.name = name
        self.engines = list(engines)
        self.sem = {e: es.enter_context(nc.semaphore(f"{name}_{e}"))
                    for e in self.engines}
        self.bodies = {}
        self.marks = {}
        self._es = es

    def body(self, engname):
        def deco(fn):
            self.bodies[engname] = fn
            return fn
        return deco

    def dma_sem(self, nm, n=1):
        if n == 1:
            return self._es.enter_context(
                self.nc.semaphore(f"{self.name}_{nm}"))
        return [self._es.enter_context(
            self.nc.semaphore(f"{self.name}_{nm}{i}")) for i in range(n)]

    def finish(self):
        class Ctx:
            def __init__(c, pb, eng_name, dry, eng):
                c.pb, c.en, c.dry, c.eng = pb, eng_name, dry, eng
                c.n = 0

            def op(c, fn):
                if not c.dry:
                    fn(c.eng).then_inc(c.pb.sem[c.en], 1)
                c.n += 1

            def opm(c, fn):
                c.op(fn)
                return c.n

            def mark(c, *key):
                if c.dry:
                    c.pb.marks[(c.en,) + key] = c.n

            def at(c, en2, *key):
                return c.pb.marks[(en2,) + key]

            def wait(c, en2, *key):
                if not c.dry:
                    v = c.pb.marks[(en2,) + key]
                    if v > 0:
                        c.eng.wait_ge(c.pb.sem[en2], v)

            def wait_n(c, en2, v):
                if not c.dry and v > 0:
                    c.eng.wait_ge(c.pb.sem[en2], v)

            def raw(c, fn):
                if not c.dry:
                    fn(c.eng)

            def dma(c, sem, fn):
                if not c.dry:
                    fn(c.eng).then_inc(sem, 16)

        for en, fn in self.bodies.items():
            fn(Ctx(self, en, True, None))
        blk = self._es.enter_context(self.nc.Block(self.name))
        for en, fn in self.bodies.items():
            def run(eng, fn=fn, en=en):
                fn(Ctx(self, en, False, eng))
            getattr(blk, en)(run)


def build_program(S):
    KC = S // 128
    NSB = S // 512

    nc = bass.Bass("TRN2", target_bir_lowering=False, debug=False)

    x_in = nc.dram_tensor("x", [S, E], F32, kind="ExternalInput").ap()
    w_in = {nm: nc.dram_tensor(nm, [E, E], F32, kind="ExternalInput").ap()
            for nm in ("wk", "wv", "wq", "wo")}
    bqd_in = nc.dram_tensor("bqd", [128, H], F32, kind="ExternalInput").ap()
    bkd_in = nc.dram_tensor("bkd", [128, NE], F32, kind="ExternalInput").ap()
    bvc_in = nc.dram_tensor("bvc", [128, NE], F32, kind="ExternalInput").ap()
    bdm_in = nc.dram_tensor("bdm", [128, 512], BF16, kind="ExternalInput").ap()
    ii_in = nc.dram_tensor("iic", [128, 64], BF16, kind="ExternalInput").ap()
    bo_in = nc.dram_tensor("bo", [1, E], F32, kind="ExternalInput").ap()
    packed_in = nc.dram_tensor("packed", [256, E], I32, kind="ExternalInput").ap()
    ws_in = nc.dram_tensor("ws", [1, 1], F32, kind="ExternalInput").ap()
    fbias_in = nc.dram_tensor("fbias", [1, E], F32, kind="ExternalInput").ap()
    ccol_in = nc.dram_tensor("ccol", [128, NE], F32, kind="ExternalInput").ap()
    wmod_in = nc.dram_tensor("wmod", [E, 6 * E], F32, kind="ExternalInput").ap()
    ident_in = nc.dram_tensor("identc", [128, 128], BF16, kind="ExternalInput").ap()
    onesb_in = nc.dram_tensor("onesbc", [128, 128], BF16, kind="ExternalInput").ap()
    onesf_in = nc.dram_tensor("onesfc", [128, 128], F32, kind="ExternalInput").ap()
    out_hbm = nc.dram_tensor("out", [S, E], F32, kind="ExternalOutput").ap()

    _scr_kind = "ExternalOutput" if DEBUG_OUTPUTS else "Internal"
    xnew_scr = nc.dram_tensor("xnew_scr", [S, E], F32, kind=_scr_kind).ap()
    o5_scr = nc.dram_tensor("o5_scr", [S * 16, 64], BF16, kind=_scr_kind)
    modrow_scr = nc.dram_tensor("modrow_scr", [1, 6 * E], F32,
                                kind=_scr_kind).ap()

    with contextlib.ExitStack() as es0:
        t = {}
        for nm, shape, dt in (
                ("ident", [128, 128], BF16), ("onesb", [128, 128], BF16),
                ("onesf", [128, 128], F32),
                ("bqd", [128, H], F32), ("bkd", [128, NE], F32),
                ("bvc", [128, NE], F32), ("bdm", [128, 512], BF16),
                ("iic", [128, 64], BF16),
                ("wssb", [1, 1], F32), ("ws_rep", [128, 1], F32),
                ("ca1_rep", [128, E], F32), ("sa_rep", [128, E], F32)):
            t[nm] = es0.enter_context(nc.sbuf_tensor("t_" + nm, shape, dt))

        with contextlib.ExitStack() as esw:
            WqD = [esw.enter_context(
                nc.sbuf_tensor(f"WqD{j}", [128, 2 * E], BF16))
                for j in range(NE)]
            WkP = [esw.enter_context(
                nc.sbuf_tensor(f"WkP{j}", [128, E], BF16))
                for j in range(NE)]
            WvT = [esw.enter_context(
                nc.sbuf_tensor(f"WvT{j}", [128, E], BF16))
                for j in range(NE)]
            WoTga = [esw.enter_context(
                nc.sbuf_tensor(f"WoTga{j}", [128, E], BF16))
                for j in range(NE)]
            gabo_bf = esw.enter_context(
                nc.sbuf_tensor("gabo_bf", [1, E], BF16))

            emit_b0(nc, t, ident_in, onesb_in, onesf_in, bqd_in,
                    bkd_in, bvc_in, ws_in, bdm_in, ii_in)
            emit_p(nc, t, w_in, ccol_in, wmod_in, bo_in, modrow_scr,
                   WqD, WkP, WvT, WoTga, gabo_bf)
            if STOP_AFTER == "b2":
                return nc
            emit_b3atn(nc, t, S, x_in, WqD, WkP, WvT, o5_scr)
            if STOP_AFTER == "a3":
                return nc
            emit_b6op(nc, t, S, x_in, o5_scr, WoTga, gabo_bf, xnew_scr)
            if STOP_AFTER == "b6":
                return nc

        emit_b7b8(nc, t, S, packed_in, fbias_in, modrow_scr, xnew_scr,
                  out_hbm)

    return nc


def emit_b0(nc, t, ident_in, onesb_in, onesf_in, bqd_in, bkd_in,
            bvc_in, ws_in, bdm_in, ii_in):
    q = Seq(nc, "B0")
    q.step("sync", lambda e: e.dma_start(out=t["ident"][:], in_=ident_in), dma=True)
    q.step("sync", lambda e: e.dma_start(out=t["onesb"][:], in_=onesb_in), dma=True)
    q.step("sync", lambda e: e.dma_start(out=t["onesf"][:], in_=onesf_in), dma=True)
    q.step("sync", lambda e: e.dma_start(out=t["bqd"][:], in_=bqd_in), dma=True)
    q.step("sync", lambda e: e.dma_start(out=t["bkd"][:], in_=bkd_in), dma=True)
    q.step("sync", lambda e: e.dma_start(out=t["bvc"][:], in_=bvc_in), dma=True)
    q.step("sync", lambda e: e.dma_start(out=t["wssb"][:], in_=ws_in), dma=True)
    q.step("sync", lambda e: e.dma_start(out=t["bdm"][:], in_=bdm_in), dma=True)
    q.step("sync", lambda e: e.dma_start(out=t["iic"][:], in_=ii_in), dma=True)
    with contextlib.ExitStack() as es:
        q.emit(es)


def emit_p(nc, t, w_in, ccol_in, wmod_in, bo_in, modrow_scr,
           WqD, WkP, WvT, WoTga, gabo_bf):
    NCH = 6 * E // 512
    worder = ("wq", "wk", "wv", "wo")
    with contextlib.ExitStack() as es:
        wstg = [es.enter_context(nc.sbuf_tensor(f"p_ws{i}", [128, E], F32))
                for i in range(2)]
        wbf = [es.enter_context(nc.sbuf_tensor(f"p_wb{i}", [128, E], BF16))
               for i in range(8)]
        wmb = [es.enter_context(nc.sbuf_tensor(f"p_wm{i}", [128, 512], F32))
               for i in range(2)]
        csil = es.enter_context(nc.sbuf_tensor("p_c", [128, NE], F32))
        ctmp = es.enter_context(nc.sbuf_tensor("p_ct", [128, NE], F32))
        modrow = es.enter_context(nc.sbuf_tensor("p_m", [1, 6 * E], F32))
        vrowS = es.enter_context(nc.sbuf_tensor("p_vs", [1, E], F32))
        vrowC = es.enter_context(nc.sbuf_tensor("p_vc", [1, E], F32))
        grow = es.enter_context(nc.sbuf_tensor("p_g", [1, E], F32))
        srow = es.enter_context(nc.sbuf_tensor("p_s", [1, E], F32))
        borow = es.enter_context(nc.sbuf_tensor("p_bo", [1, E], F32))
        garep = es.enter_context(nc.sbuf_tensor("p_gar", [128, E], F32))
        ps_mod = [es.enter_context(nc.psum_tensor(f"p_pm{i}", [1, 512], F32))
                  for i in range(2)]
        ps_bc = [es.enter_context(nc.psum_tensor(f"p_pb{i}", [128, 512],
                                                  F32)) for i in range(2)]
        ps_tr = [es.enter_context(nc.psum_tensor(f"p_pt{i}", [128, E], F32))
                 for i in range(2)]

        pb = PB(nc, es, "P", ["sync", "gpsimd", "vector", "scalar", "tensor"])
        s_wl = pb.dma_sem("wl", 2)
        s_wme = pb.dma_sem("wme")
        s_wmo = pb.dma_sem("wmo")
        s_c = pb.dma_sem("c")
        s_bo = pb.dma_sem("bo")
        s_ms = pb.dma_sem("ms")

        @pb.body("sync")
        def _(c):
            c.dma(s_c, lambda e: e.dma_start(out=csil[:], in_=ccol_in))
            c.dma(s_bo, lambda e: e.dma_start(out=borow[0:1, :], in_=bo_in))
            for wi, nm in enumerate(worder):
                for tt in range(8):
                    gi = 8 * wi + tt
                    if gi >= 2:
                        c.wait("vector", "cv", gi - 2)
                    c.dma(s_wl[gi % 2], lambda e, nm=nm, tt=tt, gi=gi:
                          e.dma_start(out=wstg[gi % 2][:],
                                      in_=w_in[nm][128 * tt:128 * (tt + 1), :]))

        @pb.body("gpsimd")
        def _(c):
            for idx in range(0, NCH * NE, 2):
                if idx >= 2:
                    c.wait("tensor", "mm", idx - 2)
                ch, e_ = idx // NE, idx % NE
                c.dma(s_wme, lambda e, ch=ch, e_=e_:
                      e.dma_start(out=wmb[0][:],
                                  in_=wmod_in[128 * e_:128 * (e_ + 1),
                                              512 * ch:512 * (ch + 1)]))

        @pb.body("tensor")
        def _(c):
            def modmm(idx):
                ch, e_ = idx // NE, idx % NE
                if idx % 2 == 0:
                    c.raw(lambda e, idx=idx: e.wait_ge(
                        s_wme, 16 * (idx // 2 + 1)))
                else:
                    c.raw(lambda e, idx=idx: e.wait_ge(
                        s_wmo, 16 * (idx // 2 + 1)))
                if e_ == 0 and ch >= 2:
                    c.wait("vector", "mr", ch - 2)
                if e_ == 0:
                    c.raw(lambda e: e.wait_ge(s_c, 16))
                    c.wait("vector", "silu")
                c.op(lambda e, ch=ch, e_=e_, idx=idx: e.matmul(
                    ps_mod[ch % 2][0:1, :], csil[:, e_:e_ + 1],
                    wmb[idx % 2][:], start=(e_ == 0), stop=(e_ == NE - 1)))
                c.mark("mm", idx)

            def trs_group(wi, j):
                gi = 8 * wi
                c.wait("vector", "cv", 8 * wi + 7)
                if j >= 2:
                    c.wait("scalar", "we", 8 * wi + j - 2)

                def trs(e, j=j):
                    for tt in range(8):
                        mm = e.matmul(
                            ps_tr[j % 2][:, 128 * tt:128 * (tt + 1)],
                            wbf[tt][:, 128 * j:128 * (j + 1)],
                            t["ident"][:], start=True, stop=True)
                    return mm
                c.op(trs)
                c.mark("tr", 8 * wi + j)

            for idx in range(48):
                modmm(idx)
            bcsrc = [(vrowS, 0, "sap"), (vrowS, 1, None),
                     (vrowC, 0, "cap"), (vrowC, 1, None),
                     (t["wssb"], None, None),
                     (grow, 0, "gap"), (grow, 1, None)]
            for k, (row, h, pre) in enumerate(bcsrc):
                if pre is not None:
                    c.wait("vector", pre)
                if k >= 2:
                    c.wait("vector", "bce", k - 2)
                if h is None:
                    c.op(lambda e, k=k: e.matmul(
                        ps_bc[k % 2][:, 0:1], t["onesf"][0:1, :],
                        t["wssb"][0:1, 0:1], start=True, stop=True))
                else:
                    c.op(lambda e, row=row, h=h, k=k: e.matmul(
                        ps_bc[k % 2][:, 0:512], t["onesf"][0:1, :],
                        row[0:1, 512 * h:512 * (h + 1)], start=True,
                        stop=True))
                c.mark("bc", k)
            for j in range(NE):
                trs_group(0, j)
            for idx in range(48, 64):
                modmm(idx)
            for j in range(NE):
                trs_group(1, j)
            for idx in range(64, 80):
                modmm(idx)
            for j in range(NE):
                trs_group(2, j)
            for idx in range(80, 96):
                modmm(idx)
            for j in range(NE):
                trs_group(3, j)

        @pb.body("scalar")
        def _(c):
            c.raw(lambda e: e.wait_ge(s_c, 16))
            c.op(lambda e: e.activation(ctmp[:], csil[:], AF.Exp, bias=0.0,
                                        scale=-1.0))
            c.mark("exp")

            def odma(idx):
                if idx >= 2:
                    c.wait("tensor", "mm", idx - 2)
                ch, e_ = idx // NE, idx % NE
                c.dma(s_wmo, lambda e, ch=ch, e_=e_:
                      e.dma_start(out=wmb[1][:],
                                  in_=wmod_in[128 * e_:128 * (e_ + 1),
                                              512 * ch:512 * (ch + 1)]))

            def evac(wi, j, kind):
                c.wait("tensor", "tr", 8 * wi + j)
                if kind == "q":
                    for rep in range(2):
                        def qev(e, j=j, rep=rep):
                            dst = bass.AP(WqD[j], 64 * rep,
                                          [[2 * E, 128], [128, H], [1, 64]])
                            return e.activation(dst, ps_tr[j % 2][:],
                                                AF.Identity, bias=0.0,
                                                scale=1.0)
                        c.op(qev)
                elif kind == "k":
                    c.op(lambda e, j=j: e.copy(WkP[j][:], ps_tr[j % 2][:]))
                elif kind == "v":
                    c.op(lambda e, j=j: e.copy(WvT[j][:], ps_tr[j % 2][:]))
                else:
                    c.op(lambda e, j=j: e.copy(WoTga[j][:], ps_tr[j % 2][:]))
                c.mark("we", 8 * wi + j)

            for idx in range(1, 48, 2):
                odma(idx)
            for j in range(NE):
                evac(0, j, "q")
            for idx in range(49, 64, 2):
                odma(idx)
            for j in range(NE):
                evac(1, j, "k")
            for idx in range(65, 80, 2):
                odma(idx)
            for j in range(NE):
                evac(2, j, "v")
            for idx in range(81, 96, 2):
                odma(idx)
            for j in range(NE):
                evac(3, j, "o")
            c.wait("vector", "mr", NCH - 1)
            c.dma(s_ms, lambda e: e.dma_start(out=modrow_scr,
                                              in_=modrow[0:1, :]))
            c.raw(lambda e: e.wait_ge(s_ms, 16))

        @pb.body("vector")
        def _(c):
            c.wait("scalar", "exp")
            c.op(lambda e: e.tensor_scalar(ctmp[:], ctmp[:], 1.0, None,
                                           ALU.add))
            c.wait_n("vector", c.n)
            c.op(lambda e: e.reciprocal(ctmp[:], ctmp[:]))
            c.wait_n("vector", c.n)
            c.op(lambda e: e.tensor_mul(csil[:], csil[:], ctmp[:]))
            c.mark("silu")

            def cast(gi):
                c.raw(lambda e, gi=gi: e.wait_ge(s_wl[gi % 2],
                                                 16 * (gi // 2 + 1)))
                wi, tt = gi // 8, gi % 8
                if wi >= 1:
                    c.wait("tensor", "tr", 8 * (wi - 1) + 7)
                c.op(lambda e, gi=gi, tt=tt: e.tensor_copy(
                    wbf[tt][:], wstg[gi % 2][:]))
                c.mark("cv", gi)

            def mr(ch):
                c.wait("tensor", "mm", 8 * ch + 7)
                c.op(lambda e, ch=ch: e.tensor_copy(
                    modrow[0:1, 512 * ch:512 * (ch + 1)],
                    ps_mod[ch % 2][0:1, :]))
                c.mark("mr", ch)

            for gi in range(8):
                cast(gi)
            for ch in range(6):
                mr(ch)
            c.wait_n("vector", c.n)
            c.op(lambda e: e.tensor_copy(vrowS[0:1, :], modrow[0:1, 0:E]))
            c.mark("sap")
            c.op(lambda e: e.tensor_scalar(vrowC[0:1, :],
                                           modrow[0:1, E:2 * E], 1.0, None,
                                           ALU.add))
            c.mark("cap")
            c.op(lambda e: e.tensor_scalar(srow[0:1, :],
                                           modrow[0:1, 2 * E:3 * E], 0.0,
                                           None, ALU.is_ge))
            c.wait_n("vector", c.n)
            c.op(lambda e: e.tensor_scalar(srow[0:1, :], srow[0:1, :], 2.0,
                                           -1.0, ALU.mult, ALU.add))
            c.op(lambda e: e.tensor_scalar(grow[0:1, :],
                                           modrow[0:1, 2 * E:3 * E], -1.0,
                                           None, ALU.mult))
            c.wait_n("vector", c.n)
            c.op(lambda e: e.tensor_max(grow[0:1, :], grow[0:1, :],
                                        modrow[0:1, 2 * E:3 * E]))
            c.wait_n("vector", c.n)
            c.op(lambda e: e.tensor_scalar(grow[0:1, :], grow[0:1, :], 0.25,
                                           None, ALU.max))
            c.wait_n("vector", c.n)
            c.op(lambda e: e.tensor_mul(grow[0:1, :], grow[0:1, :],
                                        srow[0:1, :]))
            c.mark("gap")
            bcdst = [(t["sa_rep"], 0), (t["sa_rep"], 1),
                     (t["ca1_rep"], 0), (t["ca1_rep"], 1),
                     (t["ws_rep"], None), (garep, 0), (garep, 1)]
            for k, (dst, h) in enumerate(bcdst):
                c.wait("tensor", "bc", k)
                if h is None:
                    c.op(lambda e, k=k: e.tensor_copy(t["ws_rep"][:],
                                                      ps_bc[k % 2][:, 0:1]))
                else:
                    c.op(lambda e, dst=dst, h=h, k=k: e.tensor_copy(
                        dst[:, 512 * h:512 * (h + 1)],
                        ps_bc[k % 2][:, 0:512]))
                c.mark("bce", k)
            for gi in range(8, 16):
                cast(gi)
            for ch in range(6, 8):
                mr(ch)
            c.raw(lambda e: e.wait_ge(s_bo, 16))
            c.op(lambda e: e.tensor_mul(srow[0:1, :], borow[0:1, :],
                                        grow[0:1, :]))
            c.wait_n("vector", c.n)
            c.op(lambda e: e.tensor_copy(gabo_bf[0:1, :], srow[0:1, :]))
            for gi in range(16, 24):
                cast(gi)
            for ch in range(8, 10):
                mr(ch)
            for gi in range(24, 32):
                cast(gi)
            for ch in range(10, NCH):
                mr(ch)
            for j in range(NE):
                c.wait("scalar", "we", 24 + j)
                c.op(lambda e, j=j: e.tensor_mul(WoTga[j][:], WoTga[j][:],
                                                 garep[:]))

        pb.finish()



def emit_b3atn(nc, t, S, x_in, WqD, WkP, WvT, o5_scr):
    NSB = S // 512
    if A3_SB is not None:
        NSB = min(A3_SB, NSB)
    NU = 48 if A3_PHASE >= 5 else 32
    with contextlib.ExitStack() as es:
        xb = [es.enter_context(nc.sbuf_tensor(f"a3x{i}", [128, E], F32))
              for i in range(2)]
        hb = [es.enter_context(nc.sbuf_tensor(f"a3h{i}", [128, E], BF16))
              for i in range(4)]
        hT = [es.enter_context(nc.sbuf_tensor(f"a3ht{j}", [128, 512], BF16))
              for j in range(NE)]
        lnx = [es.enter_context(nc.sbuf_tensor(f"a3ln{i}", [128, E], F32))
               for i in range(2)]
        sqd = [es.enter_context(nc.sbuf_tensor(f"a3sqd{i}", [128, E], BF16))
               for i in range(2)]
        st_ = [es.enter_context(nc.sbuf_tensor(f"a3st{i}", [128, 12], F32))
               for i in range(2)]
        QD = es.enter_context(nc.sbuf_tensor("a3qd", [128, 16 * 512], BF16))
        KS = es.enter_context(nc.sbuf_tensor("a3ks", [128, 8 * 512], BF16))
        VP = es.enter_context(nc.sbuf_tensor("a3vp", [128, 8 * 512], BF16))
        eS = [es.enter_context(nc.sbuf_tensor(f"a3es{i}", [128, 512], BF16))
              for i in range(16)]
        V4 = [es.enter_context(nc.sbuf_tensor(f"a3v4_{i}", [128, 260], BF16))
              for i in range(16)]
        o5s = [es.enter_context(nc.sbuf_tensor(f"a3o5_{i}", [128, 512], BF16))
               for i in range(2)]
        rec = es.enter_context(nc.sbuf_tensor("a3rec", [128, 16], F32))
        ps_tr = [es.enter_context(nc.psum_tensor(f"a3pt{i}", [128, 512], F32))
                 for i in range(2)]
        ps_a = [es.enter_context(nc.psum_tensor(f"a3pa{i}", [128, 512], F32))
                for i in range(2)]
        ps_g = [es.enter_context(nc.psum_tensor(f"a3pg{i}", [128, 512], F32))
                for i in range(2)]
        ps_av = [es.enter_context(nc.psum_tensor(f"a3pv{i}", [128, 260], F32))
                 for i in range(2)]

        pb = PB(nc, es, "A3", ["sync", "gpsimd", "vector", "scalar", "tensor"])
        sx = pb.dma_sem("sx", 2)
        so5 = pb.dma_sem("so5", 2)

        def pa_user(sb, u):
            if u < 16:
                return ("scalar", ("pe", 48 * sb + u))
            if u < 32:
                return ("vector", ("ve", 48 * sb + u))
            return ("vector", ("v4e", 48 * sb + u))

        @pb.body("sync")
        def _(c):
            for sb in range(NSB):
                for tt in range(4):
                    xi = 4 * sb + tt
                    if xi >= 2:
                        c.wait("scalar", "lnx", xi - 2)
                        c.wait("vector", "v1", xi - 2)
                    c.dma(sx[xi % 2], lambda e, sb=sb, tt=tt, xi=xi:
                          e.dma_start(out=xb[xi % 2][:],
                                      in_=x_in[512 * sb + 128 * tt:
                                               512 * sb + 128 * (tt + 1), :]))

        @pb.body("tensor")
        def _(c):
            if A3_PHASE < 2:
                return
            for sb in range(NSB):
                for j in range(NE):
                    gj = 8 * sb + j
                    c.wait("vector", "h", 4 * sb + 3)
                    if gj >= 2:
                        c.wait("scalar", "he", gj - 2)

                    def trs(e, j=j):
                        for tt in range(4):
                            mm = e.matmul(
                                ps_tr[j % 2][:, 128 * tt:128 * (tt + 1)],
                                hb[tt][:, 128 * j:128 * (j + 1)],
                                t["ident"][:], start=True, stop=True)
                        return mm
                    c.op(trs)
                    c.mark("tp", gj)
                c.wait("scalar", "he", 8 * sb + 7)
                if A3_PHASE < 3:
                    continue
                for u in range(32):
                    gu = 48 * sb + u
                    pos = NU * sb + u
                    if pos >= 2:
                        psb, puu = divmod(pos - 2, NU)
                        en, key = pa_user(psb, puu)
                        c.wait(en, *key)
                    if u < 16:
                        hh = u

                        def qmm(e, hh=hh, gu=gu):
                            for e_ in range(NE):
                                mm = e.matmul(
                                    ps_a[gu % 2][:],
                                    WqD[e_][:, 128 * hh:128 * (hh + 1)],
                                    hT[e_][:], start=(e_ == 0),
                                    stop=(e_ == NE - 1))
                            return mm
                        c.op(qmm)
                    elif u < 24:
                        pr = u - 16

                        def kmm(e, pr=pr, gu=gu):
                            for e_ in range(NE):
                                mm = e.matmul(
                                    ps_a[gu % 2][:],
                                    WkP[e_][:, 128 * pr:128 * (pr + 1)],
                                    hT[e_][:], start=(e_ == 0),
                                    stop=(e_ == NE - 1))
                            return mm
                        c.op(kmm)
                    else:
                        hp = u - 24

                        def vmm(e, hp=hp, gu=gu):
                            for e_ in range(NE):
                                mm = e.matmul(
                                    ps_a[gu % 2][:],
                                    WvT[e_][:, 128 * hp:128 * (hp + 1)],
                                    hT[e_][:], start=(e_ == 0),
                                    stop=(e_ == NE - 1))
                            return mm
                        c.op(vmm)
                    c.mark("pm", gu)
                if A3_PHASE < 4:
                    continue
                c.wait("scalar", "pe", 48 * sb + 15)
                c.wait("vector", "ve", 48 * sb + 23)
                for gb in range(16):
                    gg = 16 * sb + gb
                    if gg >= 2:
                        c.wait("scalar", "ex", gg - 2)

                    def gmm(e, sb=sb, gb=gb, gg=gg):
                        for g2 in range(4):
                            gloc = 4 * gb + g2
                            e.matmul(
                                ps_g[gg % 2][0:64, 128 * g2:128 * (g2 + 1)],
                                KS[0:64, 64 * gloc:64 * (gloc + 1)],
                                QD[0:64, 128 * gloc:128 * (gloc + 1)],
                                start=True, stop=True)
                            mm = e.matmul(
                                ps_g[gg % 2][64:128, 128 * g2:128 * (g2 + 1)],
                                KS[64:128, 64 * gloc:64 * (gloc + 1)],
                                QD[64:128, 128 * gloc:128 * (gloc + 1)],
                                start=True, stop=True,
                                tile_position=(64, 64))
                        return mm
                    c.op(gmm)
                    c.mark("gm", gg)
                if A3_PHASE < 5:
                    continue
                c.wait("vector", "ve", 48 * sb + 31)
                for vb in range(16):
                    u = 32 + vb
                    gu = 48 * sb + u
                    pu = gu - 2
                    psb, puu = divmod(pu, 48)
                    en, key = pa_user(psb, puu)
                    c.wait(en, *key)

                    def v4mm(e, sb=sb, vb=vb, gu=gu):
                        for g2 in range(4):
                            gloc = 4 * vb + g2
                            for par in range(2):
                                lhs = bass.AP(VP, 64 * par * 4096 + 64 * gloc,
                                              [[4096, 64], [1, 64]])
                                mm = e.matmul(
                                    ps_a[gu % 2][64 * par:64 * (par + 1),
                                                 64 * g2:64 * (g2 + 1)],
                                    lhs, t["iic"][64 * par:64 * (par + 1), :],
                                    start=True, stop=True,
                                    tile_position=(64 * par, 64 * par))
                        return mm
                    c.op(v4mm)
                    c.mark("pm", gu)
                if A3_PHASE < 6:
                    continue
                for ab in range(16):
                    ga_ = 16 * sb + ab
                    gb = ab // 1
                    if ga_ >= 2 and A3_PHASE >= 8:
                        c.wait("scalar", "nm", ga_ - 2)
                    c.wait("vector", "mk", 16 * sb + ab)
                    c.wait("vector", "v4e", 48 * sb + 32 + ab)

                    def avmm(e, sb=sb, ab=ab, ga_=ga_):
                        for g2 in range(4):
                            mm = e.matmul(
                                ps_av[ga_ % 2][:, 65 * g2:65 * (g2 + 1)],
                                eS[ab][:, 128 * g2:128 * (g2 + 1)],
                                V4[ab][:, 65 * g2:65 * g2 + 65],
                                start=True, stop=True)
                        return mm
                    c.op(avmm)
                    c.mark("av", ga_)

        @pb.body("scalar")
        def _(c):
            for sb in range(NSB):
                for tt in range(4):
                    xi = 4 * sb + tt
                    c.raw(lambda e, xi=xi: e.wait_ge(sx[xi % 2],
                                                     16 * (xi // 2 + 1)))
                    if xi >= 2:
                        c.wait("scalar", "sq", xi - 2)
                        c.wait("vector", "v2", xi - 2)
                    c.op(lambda e, xi=xi: e.activation(
                        sqd[xi % 2][:], xb[xi % 2][:], AF.Square, bias=0.0,
                        scale=1.0, accum_out=st_[xi % 2][:, 1:2]))
                    c.mark("sq", xi)
                    c.wait("vector", "v2", xi)
                    c.op(lambda e, xi=xi: e.activation(
                        st_[xi % 2][:, 6:7], st_[xi % 2][:, 5:6], AF.Ln,
                        bias=0.0, scale=1.0))
                    c.wait_n("scalar", c.n)
                    c.op(lambda e, xi=xi: e.activation(
                        st_[xi % 2][:, 7:8], st_[xi % 2][:, 6:7], AF.Exp,
                        bias=0.0, scale=-0.5))
                    c.mark("rstd", xi)
                    c.wait("vector", "v3", xi)
                    if xi >= 2:
                        c.wait("vector", "h", xi - 2)
                    c.op(lambda e, xi=xi: e.activation(
                        lnx[xi % 2][:], xb[xi % 2][:], AF.Identity,
                        bias=st_[xi % 2][:, 9:10], scale=st_[xi % 2][:, 7:8]))
                    c.mark("lnx", xi)
                if A3_PHASE < 2:
                    continue
                for j in range(NE):
                    gj = 8 * sb + j
                    c.wait("tensor", "tp", gj)
                    c.op(lambda e, j=j: e.copy(hT[j][:], ps_tr[j % 2][:]))
                    c.mark("he", gj)
                if A3_PHASE < 3:
                    continue
                for u in range(16):
                    gu = 48 * sb + u
                    c.wait("tensor", "pm", gu)
                    hh = u
                    off = (hh % 2) * 64 + (hh // 2) * 8

                    def pevac(e, off=off, hh=hh, gu=gu):
                        dst = bass.AP(QD, off,
                                      [[16 * 512, 128], [128, 64], [1, 8]])
                        return e.activation(dst, ps_a[gu % 2][:], AF.Identity,
                                            bias=t["bqd"][:, hh:hh + 1],
                                            scale=1.0)
                    c.op(pevac)
                    c.mark("pe", gu)
                if A3_PHASE < 4:
                    continue
                for gb in range(16):
                    gg = 16 * sb + gb
                    c.wait("tensor", "gm", gg)
                    if gg >= 16 and A3_PHASE >= 6:
                        c.wait("tensor", "av", gg - 16)
                    c.op(lambda e, gb=gb, gg=gg: e.activation(
                        eS[gb][:], ps_g[gg % 2][:], AF.Exp, bias=0.0,
                        scale=0.125))
                    c.mark("ex", gg)
                if A3_PHASE < 8:
                    continue
                for ab in range(16):
                    ga_ = 16 * sb + ab
                    st5 = ab // 2
                    sti = 8 * sb + st5
                    c.wait("tensor", "av", ga_)
                    c.wait("vector", "rc", ga_)
                    if ab % 2 == 0 and sti >= 2 and A3_PHASE >= 9:
                        c.raw(lambda e, sti=sti: e.wait_ge(
                            so5[sti % 2], 16 * (sti // 2)))
                    for g2 in range(4):
                        gs = (ab % 2) * 4 + g2
                        c.op(lambda e, ab=ab, g2=g2, ga_=ga_, sti=sti, gs=gs:
                             e.activation(
                                 o5s[sti % 2][:, 64 * gs:64 * (gs + 1)],
                                 ps_av[ga_ % 2][:, 65 * g2:65 * g2 + 64],
                                 AF.Identity, bias=0.0,
                                 scale=rec[:, 4 * (ab % 4) + g2:
                                           4 * (ab % 4) + g2 + 1]))
                    c.mark("nm", ga_)

        @pb.body("vector")
        def _(c):
            inv = 1.0 / E
            for i in range(16):
                c.op(lambda e, i=i: e.memset(
                    bass.AP(V4[i], 64, [[260, 128], [65, 4]]), 1.0))
            c.mark("vinit")
            for sb in range(NSB):
                for tt in range(4):
                    xi = 4 * sb + tt
                    sl = st_[xi % 2]
                    c.raw(lambda e, xi=xi: e.wait_ge(sx[xi % 2],
                                                     16 * (xi // 2 + 1)))
                    if xi >= 2:
                        c.wait("scalar", "lnx", xi - 2)
                    c.op(lambda e, xi=xi: e.reduce_sum(
                        st_[xi % 2][:, 0:1], xb[xi % 2][:], axis=AX))
                    c.mark("v1", xi)
                    c.wait("scalar", "sq", xi)
                    c.wait_n("vector", c.n)
                    c.op(lambda e, sl=sl: e.tensor_scalar(
                        sl[:, 2:3], sl[:, 0:1], inv, None, ALU.mult))
                    c.op(lambda e, sl=sl: e.tensor_scalar(
                        sl[:, 3:4], sl[:, 1:2], inv, None, ALU.mult))
                    c.wait_n("vector", c.n)
                    c.op(lambda e, sl=sl: e.tensor_mul(sl[:, 4:5], sl[:, 2:3],
                                                       sl[:, 2:3]))
                    c.wait_n("vector", c.n)
                    c.op(lambda e, sl=sl: e.tensor_sub(sl[:, 5:6], sl[:, 3:4],
                                                       sl[:, 4:5]))
                    c.wait_n("vector", c.n)
                    c.op(lambda e, sl=sl: e.tensor_scalar(
                        sl[:, 5:6], sl[:, 5:6], 1e-5, None, ALU.add))
                    c.mark("v2", xi)
                    c.wait("scalar", "rstd", xi)
                    c.op(lambda e, sl=sl: e.tensor_mul(sl[:, 8:9], sl[:, 2:3],
                                                       sl[:, 7:8]))
                    c.wait_n("vector", c.n)
                    c.op(lambda e, sl=sl: e.tensor_scalar(
                        sl[:, 9:10], sl[:, 8:9], -1.0, None, ALU.mult))
                    c.mark("v3", xi)
                    c.wait("scalar", "lnx", xi)
                    c.op(lambda e, xi=xi: e.tensor_mul(
                        lnx[xi % 2][:], lnx[xi % 2][:], t["ca1_rep"][:]))
                    c.wait_n("vector", c.n)
                    c.op(lambda e, xi=xi, tt=tt: e.tensor_add(
                        hb[tt][:], lnx[xi % 2][:], t["sa_rep"][:]))
                    c.mark("h", xi)
                if A3_PHASE < 3:
                    continue
                for pr in range(8):
                    gu = 48 * sb + 16 + pr
                    c.wait("tensor", "pm", gu)

                    def kevac(e, pr=pr, gu=gu):
                        dst = bass.AP(KS, 8 * pr,
                                      [[8 * 512, 128], [64, 64], [1, 8]])
                        return e.tensor_scalar(dst, ps_a[gu % 2][:],
                                               t["bkd"][:, pr:pr + 1], None,
                                               ALU.add)
                    c.op(kevac)
                    c.mark("ve", gu)
                for hp in range(8):
                    gu = 48 * sb + 24 + hp
                    c.wait("tensor", "pm", gu)

                    def vevac(e, hp=hp, gu=gu):
                        dst = bass.AP(VP, 8 * hp,
                                      [[8 * 512, 128], [64, 64], [1, 8]])
                        return e.tensor_scalar(dst, ps_a[gu % 2][:],
                                               t["bvc"][:, hp:hp + 1], None,
                                               ALU.add)
                    c.op(vevac)
                    c.mark("ve", gu)
                if A3_PHASE < 4:
                    continue
                for gb in range(16):
                    gg = 16 * sb + gb
                    c.wait("scalar", "ex", gg)
                    c.op(lambda e, gb=gb: e.tensor_mul(eS[gb][:], eS[gb][:],
                                                       t["bdm"][:]))
                    c.mark("mk", gg)
                if A3_PHASE < 5:
                    continue
                for vb in range(16):
                    gu = 48 * sb + 32 + vb
                    c.wait("tensor", "pm", gu)
                    if 16 * sb + vb >= 16 and A3_PHASE >= 6:
                        c.wait("tensor", "av", 16 * sb + vb - 16)

                    def v4e(e, vb=vb, gu=gu):
                        src = bass.AP(ps_a[gu % 2], 0,
                                      [[512, 128], [64, 4], [1, 64]])
                        dst = bass.AP(V4[vb], 0, [[260, 128], [65, 4], [1, 64]])
                        return e.tensor_copy(dst, src)
                    c.op(v4e)
                    c.mark("v4e", gu)
                if A3_PHASE < 7:
                    continue
                for ab in range(16):
                    ga_ = 16 * sb + ab
                    c.wait("tensor", "av", ga_)
                    if ga_ >= 4 and A3_PHASE >= 8:
                        c.wait("scalar", "nm", ga_ - 4)

                    def rcp(e, ab=ab, ga_=ga_):
                        src = bass.AP(ps_av[ga_ % 2], 64, [[260, 128], [65, 4]])
                        return e.reciprocal(rec[:, 4 * (ab % 4):
                                                4 * (ab % 4) + 4], src)
                    c.op(rcp)
                    c.mark("rc", ga_)

        @pb.body("gpsimd")
        def _(c):
            if A3_PHASE < 9:
                return
            for sb in range(NSB):
                for st5 in range(8):
                    sti = 8 * sb + st5
                    c.wait("scalar", "nm", 16 * sb + 2 * st5 + 1)

                    def o5dma(e, sb=sb, st5=st5, sti=sti):
                        dst = bass.AP(o5_scr, (64 * sb + 8 * st5) * 8192,
                                      [[64, 128], [8192, 8], [1, 64]])
                        src = bass.AP(o5s[sti % 2], 0,
                                      [[512, 128], [64, 8], [1, 64]])
                        return e.dma_start(out=dst, in_=src)
                    c.dma(so5[sti % 2], o5dma)
            c.raw(lambda e: e.wait_ge(so5[0], 16 * 4 * NSB))
            c.raw(lambda e: e.wait_ge(so5[1], 16 * 4 * NSB))

        pb.finish()


def emit_b6op(nc, t, S, x_in, o5_scr, WoTga, gabo_bf, xnew_scr):
    NT = S // 128
    with contextlib.ExitStack() as es:
        o2b = [es.enter_context(nc.sbuf_tensor(f"b6ob{i}", [128, E], BF16))
               for i in range(2)]
        o2T = [es.enter_context(nc.sbuf_tensor(f"b6oT{i}", [128, E], BF16))
               for i in range(2)]
        xt = [es.enter_context(nc.sbuf_tensor(f"b6x{i}", [128, E], F32))
              for i in range(2)]
        ps_tr = [es.enter_context(nc.psum_tensor(f"b6pt{i}", [128, 512], F32))
                 for i in range(2)]
        ps_o = [es.enter_context(nc.psum_tensor(f"b6po{i}", [128, 512], F32))
                for i in range(2)]

        pb = PB(nc, es, "B6", ["sync", "gpsimd", "vector", "scalar", "tensor"])
        ol = pb.dma_sem("ol", 2)
        xl = pb.dma_sem("xl", 2)
        xs = pb.dma_sem("xs", 2)

        R = S // 16

        @pb.body("sync")
        def _(c):
            for k in range(NT):
                if k >= 2:
                    c.wait("tensor", "tp", 2 * (k - 2) + 1)
                nsub = max(1, 128 // R)
                psz = 128 // nsub
                for a in range(nsub):
                    hq0 = (128 * k + a * psz) // R
                    n0 = (hq0 % 2) * 64 + (hq0 // 2) * 8
                    rb = (128 * k + a * psz) % R

                    def o2dma(e, k=k, a=a, psz=psz, n0=n0, rb=rb):
                        src = bass.AP(o5_scr, 16384 * rb + 64 * n0,
                                      [[16384, psz], [8192, 2], [64, 8],
                                       [1, 64]])
                        dst = bass.AP(o2b[k % 2], a * psz * E,
                                      [[E, psz], [512, 2], [64, 8], [1, 64]])
                        return e.dma_start(out=dst, in_=src)
                    c.dma(ol[k % 2], o2dma)
                if k >= 2:
                    c.wait("vector", "rs", k - 2)
                    c.raw(lambda e, k=k: e.wait_ge(xs[k % 2], 16 * (k // 2)))
                c.dma(xl[k % 2], lambda e, k=k: e.dma_start(
                    out=xt[k % 2][:],
                    in_=x_in[128 * k:128 * (k + 1), :]))

        @pb.body("tensor")
        def _(c):
            nsub = max(1, 128 // R)
            for k in range(NT):
                c.raw(lambda e, k=k, nsub=nsub: e.wait_ge(
                    ol[k % 2], 16 * nsub * (k // 2 + 1)))
                for b in range(2):
                    gb = 2 * k + b
                    if gb >= 2:
                        c.wait("scalar", "oe", gb - 2)

                    def trs(e, k=k, b=b):
                        for jj in range(4):
                            mm = e.matmul(
                                ps_tr[b % 2][:, 128 * jj:128 * (jj + 1)],
                                o2b[k % 2][:, 512 * b + 128 * jj:
                                            512 * b + 128 * (jj + 1)],
                                t["ident"][:], start=True, stop=True)
                        return mm
                    c.op(trs)
                    c.mark("tp", gb)
                c.wait("scalar", "oe", 2 * k + 1)
                if k >= 1:
                    c.wait("vector", "rs", k - 1)
                for fh in range(2):
                    def opmm(e, k=k, fh=fh):
                        for pp in range(NE):
                            e.matmul(ps_o[fh][:],
                                     o2T[k % 2][:, 128 * pp:128 * (pp + 1)],
                                     WoTga[pp][:, 512 * fh:512 * (fh + 1)],
                                     start=(pp == 0), stop=False)
                        return e.matmul(ps_o[fh][:], t["onesb"][0:1, :],
                                        gabo_bf[0:1, 512 * fh:512 * (fh + 1)],
                                        start=False, stop=True)
                    c.op(opmm)
                c.mark("op", k)

        @pb.body("scalar")
        def _(c):
            for k in range(NT):
                for b in range(2):
                    gb = 2 * k + b
                    c.wait("tensor", "tp", gb)
                    if k >= 2:
                        c.wait("tensor", "op", k - 2)
                    c.op(lambda e, k=k, b=b: e.copy(
                        o2T[k % 2][:, 512 * b:512 * (b + 1)],
                        ps_tr[b % 2][:]))
                    c.mark("oe", gb)

        @pb.body("vector")
        def _(c):
            for k in range(NT):
                c.wait("tensor", "op", k)
                c.raw(lambda e, k=k: e.wait_ge(xl[k % 2], 16 * (k // 2 + 1)))
                c.op(lambda e, k=k: e.tensor_add(
                    xt[k % 2][:, 0:512], ps_o[0][:], xt[k % 2][:, 0:512]))
                c.op(lambda e, k=k: e.tensor_add(
                    xt[k % 2][:, 512:1024], ps_o[1][:],
                    xt[k % 2][:, 512:1024]))
                c.mark("rs", k)

        @pb.body("gpsimd")
        def _(c):
            for k in range(NT):
                c.wait("vector", "rs", k)
                c.dma(xs[k % 2], lambda e, k=k: e.dma_start(
                    out=xnew_scr[128 * k:128 * (k + 1), :],
                    in_=xt[k % 2][:]))
            c.raw(lambda e: e.wait_ge(xs[0], 16 * NT // 2))
            c.raw(lambda e: e.wait_ge(xs[1], 16 * NT // 2))

        pb.finish()


def emit_b7b8(nc, t, S, packed_in, fbias_in, modrow_scr, xnew_scr, out_hbm):
    NSB = S // 512
    with contextlib.ExitStack() as es:
        wgbT = [es.enter_context(nc.sbuf_tensor(f"b8w{j}", [128, E], BF16))
                for j in range(NE)]
        cb1_rep = es.enter_context(nc.sbuf_tensor("b8cb1", [128, E], F32))
        sb_rep = es.enter_context(nc.sbuf_tensor("b8sbr", [128, E], F32))
        gbias_rep = es.enter_context(nc.sbuf_tensor("b8gbi", [128, E], F32))
        gb_rep = es.enter_context(nc.sbuf_tensor("b8gbr", [128, E], F32))

        with contextlib.ExitStack() as es7:
            pk = [es7.enter_context(nc.sbuf_tensor(f"b7p{i}", [128, E], I32))
                  for i in range(2)]
            dec = es7.enter_context(nc.sbuf_tensor("b7d", [128, E], I32))
            wdec = [es7.enter_context(nc.sbuf_tensor(f"b7wd{i}", [128, E], BF16))
                    for i in range(8)]
            vrow = es7.enter_context(nc.sbuf_tensor("b7v", [1, E], F32))
            grow = es7.enter_context(nc.sbuf_tensor("b7g", [1, E], F32))
            srow = es7.enter_context(nc.sbuf_tensor("b7s", [1, E], F32))
            frow = es7.enter_context(nc.sbuf_tensor("b7f", [1, E], F32))
            ps_bc = es7.enter_context(nc.psum_tensor("b7pb", [128, 512], F32))
            ps_tr = [es7.enter_context(nc.psum_tensor(f"b7pt{i}", [128, E],
                                                      F32)) for i in range(2)]
            q = Seq(nc, "B7")
            for pt in range(2):
                q.step("sync", lambda e, pt=pt: e.dma_start(
                    out=pk[pt][:],
                    in_=packed_in[128 * pt:128 * (pt + 1), :]), dma=True)
            for pt in range(2):
                for b in range(4):
                    oc = 2 * b + pt
                    q.step("vector", lambda e, pt=pt, b=b: e.tensor_scalar(
                        dec[:], pk[pt][:], int(2 * b), None,
                        ALU.logical_shift_right))
                    q.step("vector", lambda e: e.tensor_scalar(
                        dec[:], dec[:], 3, None, ALU.bitwise_and))
                    q.step("vector", lambda e, oc=oc: e.tensor_scalar(
                        wdec[oc][:], dec[:], -1.0, None, ALU.add))
            q.step("sync", lambda e: e.dma_start(
                out=vrow[0:1, :], in_=modrow_scr[0:1, 5 * E:6 * E]), dma=True)
            q.step("sync", lambda e: e.dma_start(out=frow[0:1, :],
                                                 in_=fbias_in), dma=True)
            q.step("vector", lambda e: e.tensor_scalar(srow[0:1, :],
                                                       vrow[0:1, :], 0.0,
                                                       None, ALU.is_ge))
            q.step("vector", lambda e: e.tensor_scalar(srow[0:1, :],
                                                       srow[0:1, :], 2.0,
                                                       -1.0, ALU.mult,
                                                       ALU.add))
            q.step("vector", lambda e: e.tensor_scalar(grow[0:1, :],
                                                       vrow[0:1, :], -1.0,
                                                       None, ALU.mult))
            q.step("vector", lambda e: e.tensor_max(grow[0:1, :],
                                                    grow[0:1, :],
                                                    vrow[0:1, :]))
            q.step("vector", lambda e: e.tensor_scalar(grow[0:1, :],
                                                       grow[0:1, :], 0.25,
                                                       None, ALU.max))
            q.step("vector", lambda e: e.tensor_mul(grow[0:1, :], grow[0:1, :],
                                                    srow[0:1, :]))
            q.step("vector", lambda e: e.tensor_mul(frow[0:1, :], frow[0:1, :],
                                                    grow[0:1, :]))
            for half in range(2):
                q.step("tensor", lambda e, h=half: e.matmul(
                    ps_bc[:, 0:512], t["onesf"][0:1, :],
                    grow[0:1, 512 * h:512 * (h + 1)], start=True, stop=True))
                q.step("vector", lambda e, h=half: e.tensor_copy(
                    gb_rep[:, 512 * h:512 * (h + 1)], ps_bc[:, 0:512]))
            for half in range(2):
                q.step("tensor", lambda e, h=half: e.matmul(
                    ps_bc[:, 0:512], t["onesf"][0:1, :],
                    frow[0:1, 512 * h:512 * (h + 1)], start=True, stop=True))
                q.step("vector", lambda e, h=half: e.tensor_copy(
                    gbias_rep[:, 512 * h:512 * (h + 1)], ps_bc[:, 0:512]))
            q.step("sync", lambda e: e.dma_start(
                out=vrow[0:1, :], in_=modrow_scr[0:1, 4 * E:5 * E]), dma=True)
            q.step("vector", lambda e: e.tensor_scalar(vrow[0:1, :],
                                                       vrow[0:1, :], 1.0,
                                                       None, ALU.add))
            for half in range(2):
                q.step("tensor", lambda e, h=half: e.matmul(
                    ps_bc[:, 0:512], t["onesf"][0:1, :],
                    vrow[0:1, 512 * h:512 * (h + 1)], start=True, stop=True))
                q.step("vector", lambda e, h=half: e.tensor_copy(
                    cb1_rep[:, 512 * h:512 * (h + 1)], ps_bc[:, 0:512]))
            q.step("sync", lambda e: e.dma_start(
                out=vrow[0:1, :], in_=modrow_scr[0:1, 3 * E:4 * E]), dma=True)
            for half in range(2):
                q.step("tensor", lambda e, h=half: e.matmul(
                    ps_bc[:, 0:512], t["onesf"][0:1, :],
                    vrow[0:1, 512 * h:512 * (h + 1)], start=True, stop=True))
                q.step("vector", lambda e, h=half: e.tensor_copy(
                    sb_rep[:, 512 * h:512 * (h + 1)], ps_bc[:, 0:512]))
            for j in range(NE):
                def trs(e, j=j):
                    for tt in range(8):
                        mm = e.matmul(
                            ps_tr[j % 2][:, 128 * tt:128 * (tt + 1)],
                            wdec[tt][:, 128 * j:128 * (j + 1)], t["ident"][:],
                            start=True, stop=True)
                    return mm
                q.step("tensor", trs)
                q.step("scalar", lambda e, j=j: e.copy(wgbT[j][:],
                                                       ps_tr[j % 2][:]))
                q.step("vector", lambda e, j=j: e.tensor_mul(
                    wgbT[j][:], wgbT[j][:], gb_rep[:]))
            q.emit(es7)

        with contextlib.ExitStack() as es8:
            xn = [es8.enter_context(nc.sbuf_tensor(f"b8x{i}", [128, E], F32))
                  for i in range(2)]
            lnx = [es8.enter_context(nc.sbuf_tensor(f"b8l{i}", [128, E], F32))
                   for i in range(2)]
            h2t = [es8.enter_context(nc.sbuf_tensor(f"b8h{i}", [128, E], F32))
                   for i in range(2)]
            pre = [es8.enter_context(nc.sbuf_tensor(f"b8pr{i}", [128, E], F32))
                   for i in range(4)]
            tq = [es8.enter_context(nc.sbuf_tensor(f"b8t{i}", [128, E], BF16))
                  for i in range(4)]
            tT = [es8.enter_context(nc.sbuf_tensor(f"b8tt{j}", [128, 512],
                                                   BF16)) for j in range(NE)]
            ytT = [es8.enter_context(nc.sbuf_tensor(f"b8y{j}", [128, 512],
                                                    BF16)) for j in range(NE)]
            ut = [es8.enter_context(nc.sbuf_tensor(f"b8u{i}", [128, E], F32))
                  for i in range(2)]
            ot_t = [es8.enter_context(nc.sbuf_tensor(f"b8ot{i}", [128, E],
                                                     F32)) for i in range(2)]
            sqd = [es8.enter_context(nc.sbuf_tensor(f"b8sq{i}", [128, E],
                                                    BF16)) for i in range(2)]
            w0 = es8.enter_context(nc.sbuf_tensor("b8z0", [128, E], F32))
            w1 = es8.enter_context(nc.sbuf_tensor("b8z1", [128, E], I32))
            w2 = es8.enter_context(nc.sbuf_tensor("b8z2", [128, E], F32))
            w3 = es8.enter_context(nc.sbuf_tensor("b8z3", [128, E], F32))
            w4 = es8.enter_context(nc.sbuf_tensor("b8z4", [128, E], F32))
            st_ = [es8.enter_context(nc.sbuf_tensor(f"b8st{i}", [128, 16],
                                                    F32)) for i in range(2)]
            fsc = es8.enter_context(nc.sbuf_tensor("b8fs", [128, 4], F32))
            ps_tr2 = [es8.enter_context(nc.psum_tensor(f"b8p2{i}", [128, 512],
                                                       F32)) for i in range(2)]
            ps_mm = [es8.enter_context(nc.psum_tensor(f"b8pm{i}", [128, 512],
                                                      F32)) for i in range(2)]
            ps_tok = [es8.enter_context(nc.psum_tensor(f"b8pk{i}", [128, E],
                                                       F32)) for i in range(2)]

            pb = PB(nc, es8, "B8",
                    ["sync", "gpsimd", "vector", "scalar", "tensor"])
            sx = pb.dma_sem("sx", 2)
            osym = pb.dma_sem("os", 2)

            @pb.body("sync")
            def _(c):
                for xi in range(4 * NSB):
                    if xi >= 2:
                        c.wait("vector", "pre", xi - 2)
                    c.dma(sx[xi % 2], lambda e, xi=xi: e.dma_start(
                        out=xn[xi % 2][:],
                        in_=xnew_scr[128 * xi:128 * (xi + 1), :]))

            @pb.body("scalar")
            def _(c):
                for sb in range(NSB):
                    for tt in range(4):
                        xi = 4 * sb + tt
                        c.raw(lambda e, xi=xi: e.wait_ge(sx[xi % 2],
                                                         16 * (xi // 2 + 1)))
                        if xi >= 2:
                            c.wait("scalar", "sq", xi - 2)
                            c.wait("vector", "v2", xi - 2)
                        c.op(lambda e, xi=xi: e.activation(
                            sqd[xi % 2][:], xn[xi % 2][:], AF.Square,
                            bias=0.0, scale=1.0,
                            accum_out=st_[xi % 2][:, 1:2]))
                        c.mark("sq", xi)
                        c.wait("vector", "v2", xi)
                        c.op(lambda e, xi=xi: e.activation(
                            st_[xi % 2][:, 6:7], st_[xi % 2][:, 5:6], AF.Ln,
                            bias=0.0, scale=1.0))
                        c.wait_n("scalar", c.n)
                        c.op(lambda e, xi=xi: e.activation(
                            st_[xi % 2][:, 7:8], st_[xi % 2][:, 6:7], AF.Exp,
                            bias=0.0, scale=-0.5))
                        c.mark("rstd", xi)
                        c.wait("vector", "v3", xi)
                        if xi >= 2:
                            c.wait("vector", "h2", xi - 2)
                        c.op(lambda e, xi=xi: e.activation(
                            lnx[xi % 2][:], xn[xi % 2][:], AF.Identity,
                            bias=st_[xi % 2][:, 9:10],
                            scale=st_[xi % 2][:, 7:8]))
                        c.mark("lnx", xi)
                    for j in range(NE):
                        gj = 8 * sb + j
                        c.wait("tensor", "tp", gj)
                        c.op(lambda e, j=j: e.copy(tT[j][:], ps_tr2[j % 2][:]))
                        c.mark("he", gj)
                    for f in range(NE):
                        gf = 8 * sb + f
                        c.wait("tensor", "fm", gf)
                        if sb >= 1:
                            c.wait("tensor", "bt", 4 * sb - 1)
                        c.op(lambda e, f=f: e.copy(ytT[f][:], ps_mm[f % 2][:]))
                        c.mark("fe", gf)
                    for tt in range(4):
                        xi = 4 * sb + tt
                        c.wait("tensor", "bt", xi)
                        c.wait("vector", "fsc", xi)
                        if xi >= 2:
                            c.wait("vector", "fin", xi - 2)
                        c.op(lambda e, xi=xi, tt=tt: e.activation(
                            ut[xi % 2][:], ps_tok[xi % 2][:], AF.Identity,
                            bias=0.0, scale=fsc[:, tt:tt + 1]))
                        c.mark("ue", xi)

            @pb.body("vector")
            def _(c):
                inv = 1.0 / E
                for sb in range(NSB):
                    for tt in range(4):
                        xi = 4 * sb + tt
                        sl = st_[xi % 2]
                        c.raw(lambda e, xi=xi: e.wait_ge(sx[xi % 2],
                                                         16 * (xi // 2 + 1)))
                        if xi >= 2:
                            c.wait("scalar", "lnx", xi - 2)
                        c.op(lambda e, xi=xi: e.reduce_sum(
                            st_[xi % 2][:, 0:1], xn[xi % 2][:], axis=AX))
                        c.mark("v1", xi)
                        c.wait("scalar", "sq", xi)
                        c.wait_n("vector", c.n)
                        c.op(lambda e, sl=sl: e.tensor_scalar(
                            sl[:, 2:3], sl[:, 0:1], inv, None, ALU.mult))
                        c.op(lambda e, sl=sl: e.tensor_scalar(
                            sl[:, 3:4], sl[:, 1:2], inv, None, ALU.mult))
                        c.wait_n("vector", c.n)
                        c.op(lambda e, sl=sl: e.tensor_mul(
                            sl[:, 4:5], sl[:, 2:3], sl[:, 2:3]))
                        c.wait_n("vector", c.n)
                        c.op(lambda e, sl=sl: e.tensor_sub(
                            sl[:, 5:6], sl[:, 3:4], sl[:, 4:5]))
                        c.wait_n("vector", c.n)
                        c.op(lambda e, sl=sl: e.tensor_scalar(
                            sl[:, 5:6], sl[:, 5:6], 1e-5, None, ALU.add))
                        c.mark("v2", xi)
                        c.wait("scalar", "rstd", xi)
                        c.op(lambda e, sl=sl: e.tensor_mul(
                            sl[:, 8:9], sl[:, 2:3], sl[:, 7:8]))
                        c.wait_n("vector", c.n)
                        c.op(lambda e, sl=sl: e.tensor_scalar(
                            sl[:, 9:10], sl[:, 8:9], -1.0, None, ALU.mult))
                        c.mark("v3", xi)
                        c.wait("scalar", "lnx", xi)
                        c.op(lambda e, xi=xi: e.tensor_mul(
                            h2t[xi % 2][:], lnx[xi % 2][:], cb1_rep[:]))
                        c.wait_n("vector", c.n)
                        c.op(lambda e, xi=xi: e.tensor_add(
                            h2t[xi % 2][:], h2t[xi % 2][:], sb_rep[:]))
                        c.mark("h2", xi)
                        c.wait_n("vector", c.n)
                        c.op(lambda e, xi=xi, sl=sl: e.reduce_max(
                            sl[:, 10:11], h2t[xi % 2][:], axis=AX,
                            apply_absolute_value=True))
                        c.wait_n("vector", c.n)
                        c.op(lambda e, sl=sl: e.tensor_scalar(
                            sl[:, 11:12], sl[:, 10:11], 1e-5, None, ALU.max))
                        c.wait_n("vector", c.n)
                        c.op(lambda e, sl=sl: e.reciprocal(
                            sl[:, 12:13], sl[:, 11:12]))
                        c.wait_n("vector", c.n)
                        c.op(lambda e, sl=sl: e.tensor_scalar(
                            sl[:, 13:14], sl[:, 12:13], 127.0, None,
                            ALU.mult))
                        c.op(lambda e, sl=sl: e.tensor_mul(
                            sl[:, 14:15], sl[:, 11:12], t["ws_rep"][:]))
                        c.wait_n("vector", c.n)
                        c.op(lambda e, sl=sl, tt=tt: e.tensor_scalar(
                            fsc[:, tt:tt + 1], sl[:, 14:15], 1.0 / 127.0,
                            None, ALU.mult))
                        c.mark("fsc", xi)
                        c.op(lambda e, xi=xi, sl=sl: e.tensor_scalar(
                            w0[:], h2t[xi % 2][:], sl[:, 13:14], None,
                            ALU.mult))
                        c.wait_n("vector", c.n)
                        c.op(lambda e: e.tensor_copy(w1[:], w0[:]))
                        c.wait_n("vector", c.n)
                        c.op(lambda e: e.tensor_copy(w2[:], w1[:]))
                        c.wait_n("vector", c.n)
                        c.op(lambda e: e.tensor_sub(w3[:], w0[:], w2[:]))
                        c.wait_n("vector", c.n)
                        c.op(lambda e: e.tensor_scalar(w4[:], w3[:], 0.5,
                                                       None, ALU.is_gt))
                        c.wait_n("vector", c.n)
                        c.op(lambda e: e.tensor_add(w2[:], w2[:], w4[:]))
                        c.wait_n("vector", c.n)
                        c.op(lambda e: e.tensor_scalar(w4[:], w3[:], -0.5,
                                                       None, ALU.is_lt))
                        c.wait_n("vector", c.n)
                        if sb >= 1:
                            c.wait("tensor", "tp", 8 * sb - 1)
                        c.op(lambda e, tt=tt: e.tensor_sub(tq[tt][:], w2[:],
                                                           w4[:]))
                        c.mark("tq", xi)
                        if sb >= 1:
                            c.wait("vector", "fin", 4 * (sb - 1) + tt)
                        c.op(lambda e, xi=xi, tt=tt: e.tensor_add(
                            pre[tt][:], xn[xi % 2][:], gbias_rep[:]))
                        c.mark("pre", xi)
                    for tt in range(4):
                        xi = 4 * sb + tt
                        c.wait("scalar", "ue", xi)
                        if xi >= 2:
                            c.raw(lambda e, xi=xi: e.wait_ge(
                                osym[xi % 2], 16 * (xi // 2)))
                        c.op(lambda e, xi=xi, tt=tt: e.tensor_add(
                            ot_t[xi % 2][:], pre[tt][:], ut[xi % 2][:]))
                        c.mark("fin", xi)

            @pb.body("tensor")
            def _(c):
                for sb in range(NSB):
                    c.wait("vector", "tq", 4 * sb + 3)
                    for j in range(NE):
                        gj = 8 * sb + j
                        if gj >= 2:
                            c.wait("scalar", "he", gj - 2)
                        def trs(e, j=j):
                            for tt in range(4):
                                mm = e.matmul(
                                    ps_tr2[j % 2][:, 128 * tt:128 * (tt + 1)],
                                    tq[tt][:, 128 * j:128 * (j + 1)],
                                    t["ident"][:], start=True, stop=True)
                            return mm
                        c.op(trs)
                        c.mark("tp", gj)
                    c.wait("scalar", "he", 8 * sb + 7)
                    for f in range(NE):
                        gf = 8 * sb + f
                        if gf >= 2:
                            c.wait("scalar", "fe", gf - 2)
                        def fmm(e, f=f):
                            for e_ in range(NE):
                                mm = e.matmul(
                                    ps_mm[f % 2][:],
                                    wgbT[e_][:, 128 * f:128 * (f + 1)],
                                    tT[e_][:], start=(e_ == 0),
                                    stop=(e_ == NE - 1))
                            return mm
                        c.op(fmm)
                        c.mark("fm", gf)
                    c.wait("scalar", "fe", 8 * sb + 7)
                    for tt in range(4):
                        xi = 4 * sb + tt
                        if xi >= 2:
                            c.wait("scalar", "ue", xi - 2)
                        def btr(e, tt=tt, xi=xi):
                            for f in range(NE):
                                mm = e.matmul(
                                    ps_tok[xi % 2][:, 128 * f:128 * (f + 1)],
                                    ytT[f][:, 128 * tt:128 * (tt + 1)],
                                    t["ident"][:], start=True, stop=True)
                            return mm
                        c.op(btr)
                        c.mark("bt", xi)

            @pb.body("gpsimd")
            def _(c):
                for xi in range(4 * NSB):
                    c.wait("vector", "fin", xi)
                    c.dma(osym[xi % 2], lambda e, xi=xi: e.dma_start(
                        out=out_hbm[128 * xi:128 * (xi + 1), :],
                        in_=ot_t[xi % 2][:]))
                c.raw(lambda e: e.wait_ge(osym[0], 16 * 2 * NSB))
                c.raw(lambda e: e.wait_ge(osym[1], 16 * 2 * NSB))

            pb.finish()


def _prep_core_inputs(b, x_full, Wq, bq, Wk, bk, Wv, bv, Wo, bo, packed, ws,
                      fbias, C, wmod):
    bf = ml_dtypes.bfloat16
    def dupb(v):
        m = v.reshape(H, 64).T
        return np.ascontiguousarray(np.vstack([m, m]))
    def pairb(v):
        return np.ascontiguousarray(
            v.reshape(NE, 2, 64).transpose(1, 2, 0).reshape(128, NE))
    bvc = pairb(bv)
    r = np.arange(128)[:, None]
    cc = np.arange(512)[None, :]
    bdm = ((r % 8) == (cc % 8)).astype(bf)
    iic = np.vstack([np.eye(64), np.eye(64)]).astype(bf)
    d = {
        "x": np.ascontiguousarray(x_full[b]),
        "wq": Wq, "wk": Wk, "wv": Wv, "wo": Wo,
        "bqd": dupb(bq), "bkd": pairb(bk), "bvc": bvc,
        "bdm": np.ascontiguousarray(bdm),
        "iic": np.ascontiguousarray(iic),
        "bo": np.ascontiguousarray(bo.reshape(1, E)),
        "packed": np.ascontiguousarray(packed.reshape(256, E)),
        "ws": np.ascontiguousarray(ws.reshape(1, 1)),
        "fbias": np.ascontiguousarray(fbias.reshape(1, E)),
        "ccol": np.ascontiguousarray(C[b].reshape(NE, 128).T),
        "wmod": wmod,
        "identc": np.eye(128, dtype=bf),
        "onesbc": np.ones((128, 128), dtype=bf),
        "onesfc": np.ones((128, 128), dtype=np.float32),
    }
    return d


def kernel(L0_attn_x, L0_attn_Wq, L0_attn_bq, L0_attn_Wk, L0_attn_bk,
           L0_attn_Wv, L0_attn_bv, L0_attn_Wo, L0_attn_bo,
           L0_ffn_packed_weight, L0_ffn_weight_scale, L0_ffn_bias,
           L0_c, L0_wmod):
    S = 4096
    x_full = np.asarray(L0_attn_x, dtype=np.float32)
    args = [np.ascontiguousarray(np.asarray(a, dtype=np.float32))
            for a in (L0_attn_Wq, L0_attn_bq, L0_attn_Wk, L0_attn_bk,
                      L0_attn_Wv, L0_attn_bv, L0_attn_Wo, L0_attn_bo)]
    Wq, bq, Wk, bk, Wv, bv, Wo, bo = args
    packed = np.ascontiguousarray(np.asarray(L0_ffn_packed_weight,
                                             dtype=np.int32))
    ws = np.asarray(L0_ffn_weight_scale, dtype=np.float32)
    fbias = np.asarray(L0_ffn_bias, dtype=np.float32)
    C = np.ascontiguousarray(np.asarray(L0_c, dtype=np.float32))
    wmod = np.ascontiguousarray(np.asarray(L0_wmod, dtype=np.float32))

    nc = build_program(S)
    in_maps = [_prep_core_inputs(b, x_full, Wq, bq, Wk, bk, Wv, bv, Wo, bo,
                                 packed, ws, fbias, C, wmod)
               for b in range(8)]
    res = run_bass_kernel_spmd(nc, in_maps, core_ids=list(range(8)))
    out = np.stack([np.asarray(res.results[b]["out"], dtype=np.float32)
                    for b in range(8)], axis=0)
    return out
